# version 01
# baseline (speedup 1.0000x reference)
import numpy as np
from contextlib import ExitStack

import concourse.bass as bass
import concourse.mybir as mybir
from concourse.bass_utils import run_bass_kernel_spmd

F32 = mybir.dt.float32
BF16 = mybir.dt.bfloat16
U32 = mybir.dt.uint32
AF = mybir.ActivationFunctionType
ALU = mybir.AluOpType

NCORES = 8
T = 1024
D = 2048
NH = 16
HD = 128
BLK = 256
NB = T // BLK
NI = 1024
NE = 32
TOPK = 4
CAP = 256
ALPHA = 7.0
EPS_LN = 1e-5
EPS_RMS = 1e-6
NT = T // 128
NKC = D // 128

BUILD_PHASES = 99
DEBUG = False


class Sem:
    def __init__(self, ctx, nc, name):
        self.h = ctx.enter_context(nc.semaphore(name))
        self.n = 0

    def inc(self, instr, k=1):
        instr.then_inc(self.h, k)
        self.n += k
        return self.n


def _wait(eng, sem, val=None):
    v = sem.n if val is None else val
    if v > 0:
        eng.wait_ge(sem.h, v)


def phase_barrier(block, sems):
    for mk in (block.sync, block.gpsimd, block.vector, block.scalar, block.tensor):
        @mk
        def _(eng):
            for s in sems:
                if s.n > 0:
                    eng.wait_ge(s.h, s.n)


def build_nc():
    nc = bass.Bass("TRN2", target_bir_lowering=False, debug=False)

    def din(name, shape, dt=F32):
        return nc.dram_tensor(name, shape, dt, kind="ExternalInput").ap()

    x_in = din("x", [T, D])
    slope = din("slope", [NH, 1])
    wq_t = din("wq_t", [D, D])
    wk_t = din("wk_t", [D, D])
    wv_t = din("wv_t", [D, D])
    wgate_t = din("wgate_t", [D, D])
    wo_t = din("wo_t", [D, D])
    normw = din("normw", [NKC, 128])
    router_t = din("router_t", [D, NE])
    ebias = din("ebias", [1, NE])
    eg_t = din("eg_t", [NE, D, NI])
    eu_t = din("eu_t", [NE, D, NI])
    ed_t = din("ed_t", [NE, NI, D])
    sg_t = din("sg_t", [D, NI])
    su_t = din("su_t", [D, NI])
    sd_t = din("sd_t", [NI, D])
    e_const = din("e_const", [NH, NCORES])

    out_tm = nc.dram_tensor("out_tm", [T, D], F32, kind="ExternalOutput").ap()

    xn_dram = nc.dram_tensor("xn_dram", [T, D], BF16, kind="Internal").ap()
    xnfm_dram = nc.dram_tensor("xnfm_dram", [D, T], BF16, kind="Internal").ap()
    q_dram = nc.dram_tensor("q_dram", [D, T], BF16, kind="Internal").ap()
    k_dram = nc.dram_tensor("k_dram", [D, T], BF16, kind="Internal").ap()
    v_dram = nc.dram_tensor("v_dram", [T, D], BF16, kind="Internal").ap()
    gate_dram = nc.dram_tensor("gate_dram", [D, T], BF16, kind="Internal").ap()
    attn_dram = nc.dram_tensor("attn_dram", [D, T], BF16, kind="Internal").ap()
    s_dram = nc.dram_tensor("s_dram", [T, D], BF16, kind="Internal").ap()
    xg_dram = nc.dram_tensor("xg_dram", [NE * CAP, D], BF16, kind="Internal").ap()
    y_dram = nc.dram_tensor("y_dram", [NE * CAP + 1, D], F32, kind="Internal").ap()
    w_dram = nc.dram_tensor("w_dram", [NH, NCORES], F32, kind="Internal").ap()
    cc_in = nc.dram_tensor("cc_in", [NH * HD, HD], BF16, kind="Internal")
    cc_out = nc.dram_tensor("cc_out", [NCORES * NH * HD, HD], BF16, kind="Internal")

    dbg = {}
    if DEBUG:
        for name, shape, dt in [
            ("dbg_xn", [T, D], BF16), ("dbg_slot", [T, NE], F32),
            ("dbg_q", [D, T], BF16), ("dbg_v", [T, D], BF16),
            ("dbg_attn", [D, T], BF16), ("dbg_s", [T, D], BF16),
            ("dbg_y", [NE * CAP + 1, D], F32), ("dbg_gate", [D, T], BF16),
            ("dbg_pj", [T, TOPK], F32), ("dbg_gidx", [T, TOPK], U32),
        ]:
            dbg[name] = nc.dram_tensor(name, shape, dt, kind="ExternalOutput").ap()

    ctx = ExitStack()
    e = ctx.enter_context

    sLD = Sem(ctx, nc, "sLD")
    sGP = Sem(ctx, nc, "sGP")
    sPE = Sem(ctx, nc, "sPE")
    sDV = Sem(ctx, nc, "sDV")
    sAC = Sem(ctx, nc, "sAC")
    sST = Sem(ctx, nc, "sST")
    sCC = Sem(ctx, nc, "sCC")
    sGI = Sem(ctx, nc, "sGI")
    ALLSEMS = [sLD, sGP, sPE, sDV, sAC, sST, sCC, sGI]

    sb = lambda name, shape, dt: e(nc.sbuf_tensor(name, shape, dt))
    ident_bf = sb("ident_bf", [128, 128], BF16)
    ident_f = sb("ident_f", [128, 128], F32)
    upper_bf = sb("upper_bf", [128, 128], BF16)
    ones_row_f = sb("ones_row_f", [1, 128], F32)
    ones_col_b = sb("ones_col_b", [128, 1], BF16)
    ones_col_f = sb("ones_col_f", [128, 1], F32)
    iota32 = sb("iota32", [128, NE], F32)
    io_row = sb("io_row", [128, 128], F32)
    io_col = sb("io_col", [128, 1], F32)
    normw_col = sb("normw_col", [128, NKC], F32)
    bias_bc = sb("bias_bc", [128, NE], F32)
    s_col = sb("s_col", [NH, 1], F32)
    s_bc = sb("s_bc", [128, NH], F32)
    bd_bc = sb("bd_bc", [128, NH], F32)
    w_bcast = sb("w_bcast", [128, 128], F32)
    slot_f = sb("slot_f", [128, NT * NE], F32)
    pj_sb = sb("pj_sb", [128, NT * TOPK], F32)
    gidx_g = sb("gidx_g", [128, NT * TOPK], U32)
    gidx_s = sb("gidx_s", [128, NT * TOPK], U32)
    var_row = sb("var_row", [1, T], F32)
    rstd_row = sb("rstd_row", [1, T], F32)
    rstd_bc = sb("rstd_bc", [128, T], BF16)
    zrow = sb("zrow", [1, D], F32)
    small = sb("small", [NH, NCORES], F32)
    wrow = sb("wrow", [1, 128], F32)
    sprow = sb("sprow", [1, NH], F32)
    bdrow = sb("bdrow", [1, NH], F32)
    nw16 = sb("nw16", [NKC, 128], F32)
    eps_col = sb("eps_col", [128, 1], F32)
    epsr_1 = sb("epsr_1", [1, 1], F32)

    psA = e(nc.psum_tensor("psA", [128, 512], F32))
    psB = e(nc.psum_tensor("psB", [128, 512], F32))
    psC = e(nc.psum_tensor("psC", [128, 512], F32))
    psD = e(nc.psum_tensor("psD", [128, 512], F32))
    psT = e(nc.psum_tensor("psT", [128, 256], BF16))
    psU = e(nc.psum_tensor("psU", [128, 128], F32))
    psO = e(nc.psum_tensor("psO", [128, 256], F32))
    psR = e(nc.psum_tensor("psR", [128, 128], F32))

    block = e(nc.Block())

    @block.gpsimd
    def _(g):
        for ap_, pat, base, cm in [
            (iota32[:, :], [[1, NE]], 0, 0),
            (io_row[:, :], [[1, 128]], 0, 0),
            (io_col[:, :], [[0, 1]], 0, 1),
        ]:
            g.iota(ap_, pattern=pat, base=base, channel_multiplier=cm,
                   allow_small_or_imprecise_dtypes=True).then_inc(sGI.h, 1)
            sGI.n += 1

    @block.sync
    def _(sy):
        sLD.inc(sy.dma_start(out=s_col[:, :], in_=slope[:, :]), 16)
        sLD.inc(sy.dma_start(out=nw16[:, :], in_=normw[:, :]), 16)
        sLD.inc(sy.dma_start(out=bias_bc[0:1, :], in_=ebias[:, :]), 16)
        sLD.inc(sy.dma_start(out=small[:, :], in_=e_const[:, :]), 16)

    @block.vector
    def _(v):
        _wait(v, sGI)
        sDV.inc(v.tensor_scalar(out=ident_bf[:, :], in0=io_row[:, :], scalar1=io_col[:, :],
                                scalar2=None, op0=ALU.is_equal))
        sDV.inc(v.tensor_scalar(out=ident_f[:, :], in0=io_row[:, :], scalar1=io_col[:, :],
                                scalar2=None, op0=ALU.is_equal))
        sDV.inc(v.tensor_scalar(out=upper_bf[:, :], in0=io_row[:, :], scalar1=io_col[:, :],
                                scalar2=None, op0=ALU.is_gt))
        sDV.inc(v.memset(ones_row_f[:, :], 1.0))
        sDV.inc(v.memset(ones_col_b[:, :], 1.0))
        sDV.inc(v.memset(ones_col_f[:, :], 1.0))
        sDV.inc(v.memset(zrow[:, :], 0.0))
        sDV.inc(v.memset(eps_col[:, :], EPS_LN))
        sDV.inc(v.memset(epsr_1[:, :], EPS_RMS))
        _wait(v, sLD, 16 * 4)
        sDV.inc(v.tensor_scalar(out=small[:, :], in0=small[:, :], scalar1=s_col[:, :],
                                scalar2=None, op0=ALU.mult))

    @block.scalar
    def _(sc):
        _wait(sc, sDV)
        sAC.inc(sc.activation(out=small[:, :], in_=small[:, :], func=AF.Exp, scale=-1.0))

    @block.sync
    def _(sy):
        _wait(sy, sAC)
        sST.inc(sy.dma_start(out=w_dram[:, :], in_=small[:, :]), 16)
        _wait(sy, sST)
        sLD.inc(sy.dma_start(
            out=wrow[:, :],
            in_=w_dram.rearrange("(one a) b -> one (a b)", one=1)), 16)
        sST.inc(sy.dma_start(out=y_dram[NE * CAP:NE * CAP + 1, :], in_=zrow[:, :]), 16)

    @block.tensor
    def _(t):
        _wait(t, sDV)
        _wait(t, sLD)
        t.transpose(psR[:, 0:NKC], nw16[:, :], ident_f[0:NKC, 0:NKC])
        sPE.inc(t.transpose(psD[0:1, 0:NH], s_col[:, :], ident_f[0:NH, 0:NH]))

    @block.scalar
    def _(sc):
        _wait(sc, sPE)
        sAC.inc(sc.activation(out=normw_col[:, :], in_=psR[:, 0:NKC], func=AF.Copy))
        sAC.inc(sc.activation(out=sprow[:, :], in_=psD[0:1, 0:NH], func=AF.Copy))
        sc.drain()
        sAC.inc(sc.activation(out=bdrow[:, :], in_=sprow[:, :], func=AF.Exp, scale=-256.0))

    @block.tensor
    def _(t):
        _wait(t, sAC)
        _wait(t, sLD)
        t.matmul(psR[:, 0:NH], ones_row_f[:, :], sprow[:, :], start=True, stop=True)
        t.matmul(psD[:, 0:NH], ones_row_f[:, :], bdrow[:, :], start=True, stop=True)
        t.matmul(psA[:, 0:NE], ones_row_f[:, :], bias_bc[0:1, :], start=True, stop=True)
        sPE.inc(t.matmul(psB[:, 0:128], ones_row_f[:, :], wrow[:, :], start=True, stop=True))

    @block.scalar
    def _(sc):
        _wait(sc, sPE)
        sAC.inc(sc.activation(out=s_bc[:, :], in_=psR[:, 0:NH], func=AF.Copy))
        sAC.inc(sc.activation(out=bd_bc[:, :], in_=psD[:, 0:NH], func=AF.Copy))
        sAC.inc(sc.activation(out=w_bcast[:, :], in_=psB[:, 0:128], func=AF.Copy))

    @block.vector
    def _(v):
        _wait(v, sPE)
        sDV.inc(v.tensor_copy(bias_bc[:, :], psA[:, 0:NE]))

    if BUILD_PHASES < 1:
        return _finish(nc, ctx, block, out_tm, x_in, sLD, sST)

    scat_mark = {}
    p1 = ExitStack()
    e1 = p1.enter_context
    x_t = [e1(nc.sbuf_tensor(f"x_t{i}", [128, D], F32)) for i in range(2)]
    xn_t = [e1(nc.sbuf_tensor(f"xn_t{i}", [128, D], BF16)) for i in range(2)]
    junk_b = e1(nc.sbuf_tensor("junk_b", [128, D], BF16))
    junk_f = e1(nc.sbuf_tensor("junk_f", [128, D], F32))
    sum_c = e1(nc.sbuf_tensor("sum_c", [128, 1], F32))
    ssq_c = e1(nc.sbuf_tensor("ssq_c", [128, 1], F32))
    mean_c = e1(nc.sbuf_tensor("mean_c", [128, 1], F32))
    rstd_c = e1(nc.sbuf_tensor("rstd_c", [128, 1], F32))
    xn_fm = e1(nc.sbuf_tensor("xn_fm", [128, NKC * T], BF16))
    rwt_sb = e1(nc.sbuf_tensor("rwt_sb", [128, NKC * NE], BF16))
    logit = e1(nc.sbuf_tensor("logit", [128, NE], F32))
    sc_abs = e1(nc.sbuf_tensor("sc_abs", [128, NE], F32))
    sc_sp = e1(nc.sbuf_tensor("sc_sp", [128, NE], F32))
    scores = e1(nc.sbuf_tensor("scores", [128, NE], F32))
    biased = e1(nc.sbuf_tensor("biased", [128, NE], F32))
    v8 = e1(nc.sbuf_tensor("v8", [128, 8], F32))
    i8 = e1(nc.sbuf_tensor("i8", [128, 8], U32))
    i8f = e1(nc.sbuf_tensor("i8f", [128, 8], F32))
    onehot = e1(nc.sbuf_tensor("onehot", [128, NE], F32))
    topv = e1(nc.sbuf_tensor("topv", [128, TOPK], F32))
    denom = e1(nc.sbuf_tensor("denom", [128, 1], F32))
    mk_b = e1(nc.sbuf_tensor("mk_b", [128, NE], BF16))
    mk_f = e1(nc.sbuf_tensor("mk_f", [128, NE], F32))
    pfx = e1(nc.sbuf_tensor("pfx", [1, NE], F32))
    tmp32 = e1(nc.sbuf_tensor("tmp32", [128, NE], F32))
    col1 = e1(nc.sbuf_tensor("col1", [128, 1], F32))
    col2 = e1(nc.sbuf_tensor("col2", [128, 1], F32))
    col3 = e1(nc.sbuf_tensor("col3", [128, 1], F32))
    col4 = e1(nc.sbuf_tensor("col4", [128, 1], F32))

    breg = {}

    @block.gpsimd
    def _(g):
        breg["scat"] = g.to_reg(NE * CAP - 1)
        breg["gath"] = g.to_reg(NE * CAP)
        sGP.inc(g.dma_start(out=rwt_sb.ap().rearrange("p (kc c) -> p kc c", kc=NKC),
                            in_=router_t.rearrange("(kc p) c -> p kc c", p=128)), 16)
    rwt_ready = sGP.n

    @block.vector
    def _(v):
        sDV.inc(v.memset(pfx[:, :], 0.0))

    for tt in range(NT):
        pi = tt % 2

        @block.sync
        def _(sy, tt=tt, pi=pi):
            if tt >= 2:
                _wait(sy, sAC)
                _wait(sy, sDV)
            sLD.inc(sy.dma_start(out=x_t[pi][:, :], in_=x_in[tt * 128:(tt + 1) * 128, :]), 16)

        @block.scalar
        def _(sc, tt=tt, pi=pi):
            _wait(sc, sLD)
            if tt >= 1:
                _wait(sc, sDV)
            sAC.inc(sc.activation(out=junk_b[:, :], in_=x_t[pi][:, :], func=AF.Copy,
                                  accum_out=sum_c[:, :]))
            sAC.inc(sc.activation(out=junk_b[:, :], in_=x_t[pi][:, :], func=AF.Square,
                                  accum_out=ssq_c[:, :]))

        @block.vector
        def _(v, tt=tt, pi=pi):
            _wait(v, sLD)
            if tt >= 2:
                v.wait_ge(sGP.h, scat_mark[tt - 2])
            _wait(v, sAC)
            sDV.inc(v.tensor_scalar(out=mean_c[:, :], in0=sum_c[:, :], scalar1=1.0 / D,
                                    scalar2=None, op0=ALU.mult))
            sDV.inc(v.tensor_scalar(out=col1[:, :], in0=ssq_c[:, :], scalar1=1.0 / D,
                                    scalar2=None, op0=ALU.mult))
            v.drain()
            sDV.inc(v.tensor_tensor(out=col2[:, :], in0=mean_c[:, :], in1=mean_c[:, :],
                                    op=ALU.mult))
            v.drain()
            sDV.inc(v.tensor_tensor(out=col3[:, :], in0=col1[:, :], in1=col2[:, :],
                                    op=ALU.subtract))
            v.drain()

        @block.scalar
        def _(sc, tt=tt):
            _wait(sc, sDV)
            sAC.inc(sc.activation(out=col4[:, :], in_=col3[:, :], func=AF.Sqrt, bias=eps_col[:, :]))

        @block.vector
        def _(v, tt=tt, pi=pi):
            _wait(v, sAC)
            sDV.inc(v.reciprocal(rstd_c[:, :], col4[:, :]))
            v.drain()
            sDV.inc(v.tensor_scalar(out=xn_t[pi][:, :], in0=x_t[pi][:, :],
                                    scalar1=mean_c[:, :], scalar2=rstd_c[:, :],
                                    op0=ALU.subtract, op1=ALU.mult))

        @block.sync
        def _(sy, tt=tt, pi=pi):
            _wait(sy, sDV)
            sST.inc(sy.dma_start(out=xn_dram[tt * 128:(tt + 1) * 128, :], in_=xn_t[pi][:, :]), 16)

        @block.tensor
        def _(t, tt=tt, pi=pi):
            _wait(t, sDV)
            for kc in range(NKC):
                half = kc % 2
                _wait(t, sAC)
                sPE.inc(t.transpose(psT[:, half * 128:(half + 1) * 128],
                                    xn_t[pi][:, kc * 128:(kc + 1) * 128], ident_bf[:, :]))

                @block.scalar
                def _(sc, tt=tt, kc=kc, half=half):
                    _wait(sc, sPE)
                    sAC.inc(sc.activation(
                        out=xn_fm[:, kc * T + tt * 128: kc * T + (tt + 1) * 128],
                        in_=psT[:, half * 128:(half + 1) * 128], func=AF.Copy))

        @block.tensor
        def _(t, tt=tt):
            _wait(t, sAC)
            _wait(t, sDV)
            _wait(t, sGP, rwt_ready)
            for kc in range(NKC):
                sPE.inc(t.matmul(psR[:, 0:NE],
                                 xn_fm[:, kc * T + tt * 128: kc * T + (tt + 1) * 128],
                                 rwt_sb[:, kc * NE:(kc + 1) * NE],
                                 start=(kc == 0), stop=(kc == NKC - 1)))

        @block.vector
        def _(v, tt=tt):
            _wait(v, sPE)
            sDV.inc(v.tensor_copy(logit[:, :], psR[:, 0:NE]))

        @block.scalar
        def _(sc, tt=tt):
            _wait(sc, sDV)
            sAC.inc(sc.activation(out=sc_abs[:, :], in_=logit[:, :], func=AF.Abs))
            sc.drain()
            sAC.inc(sc.activation(out=sc_sp[:, :], in_=sc_abs[:, :], func=AF.Exp, scale=-1.0))
            sc.drain()
            sAC.inc(sc.activation(out=sc_sp[:, :], in_=sc_sp[:, :], func=AF.Ln, bias=1.0))

        @block.vector
        def _(v, tt=tt):
            _wait(v, sAC)
            sDV.inc(v.tensor_scalar(out=sc_abs[:, :], in0=logit[:, :], scalar1=0.0,
                                    scalar2=None, op0=ALU.max))
            v.drain()
            sDV.inc(v.tensor_tensor(out=sc_sp[:, :], in0=sc_sp[:, :], in1=sc_abs[:, :],
                                    op=ALU.add))

        @block.scalar
        def _(sc, tt=tt):
            _wait(sc, sDV)
            sAC.inc(sc.activation(out=scores[:, :], in_=sc_sp[:, :], func=AF.Sqrt))

        @block.vector
        def _(v, tt=tt):
            _wait(v, sAC)
            sDV.inc(v.tensor_tensor(out=biased[:, :], in0=scores[:, :], in1=bias_bc[:, :],
                                    op=ALU.add))
            v.drain()
            v.max(v8[:, :], biased[:, :])
            v.drain()
            sDV.inc(v.max_index(i8[:, :], v8[:, :], biased[:, :]))
            v.drain()
            sDV.inc(v.tensor_scalar(out=i8f[:, :], in0=i8[:, :], scalar1=0.0,
                                    scalar2=None, op0=ALU.add))
            sDV.inc(v.tensor_scalar(out=mk_f[:, :], in0=biased[:, :], scalar1=v8[:, 3:4],
                                    scalar2=None, op0=ALU.is_ge))
            v.drain()
            sDV.inc(v.tensor_copy(mk_b[:, :], mk_f[:, :]))
            for j in range(TOPK):
                sDV.inc(v.tensor_scalar(out=onehot[:, :], in0=iota32[:, :],
                                        scalar1=i8f[:, j:j + 1], scalar2=None,
                                        op0=ALU.is_equal))
                v.drain()
                sDV.inc(v.scalar_tensor_tensor(out=tmp32[:, :], in0=onehot[:, :], scalar=1.0,
                                               in1=scores[:, :], op0=ALU.mult, op1=ALU.mult,
                                               accum_out=topv[:, j:j + 1]))
                v.drain()
            sDV.inc(v.tensor_reduce(out=denom[:, :], in_=topv[:, 0:TOPK],
                                    axis=mybir.AxisListType.X, op=ALU.add))
            v.drain()
            sDV.inc(v.tensor_scalar(out=denom[:, :], in0=denom[:, :], scalar1=1e-20,
                                    scalar2=None, op0=ALU.add))
            v.drain()
            sDV.inc(v.reciprocal(col1[:, :], denom[:, :]))
            v.drain()
            for j in range(TOPK):
                sDV.inc(v.tensor_scalar(out=pj_sb[:, tt * TOPK + j:tt * TOPK + j + 1],
                                        in0=topv[:, j:j + 1], scalar1=col1[:, :],
                                        scalar2=None, op0=ALU.mult))

        @block.tensor
        def _(t, tt=tt):
            _wait(t, sDV)
            t.matmul(psR[:, 0:NE], upper_bf[:, :], mk_b[:, :], start=True, stop=False)
            sPE.inc(t.matmul(psR[:, 0:NE], ones_row_f[:, :], pfx[:, :], start=False, stop=True))
            sPE.inc(t.matmul(psD[0:1, 0:NE], ones_col_b[:, :], mk_b[:, :], start=True, stop=True))

        @block.vector
        def _(v, tt=tt):
            _wait(v, sPE)
            sDV.inc(v.tensor_scalar(out=tmp32[:, :], in0=mk_f[:, :], scalar1=-1e9,
                                    scalar2=1e9, op0=ALU.mult, op1=ALU.add))
            v.drain()
            sDV.inc(v.tensor_tensor(out=slot_f[:, tt * NE:(tt + 1) * NE], in0=psR[:, 0:NE],
                                    in1=tmp32[:, :], op=ALU.add))
            sDV.inc(v.tensor_tensor(out=pfx[:, :], in0=pfx[:, :], in1=psD[0:1, 0:NE],
                                    op=ALU.add))
            v.drain()
            for j in range(TOPK):
                sDV.inc(v.tensor_scalar(out=onehot[:, :], in0=iota32[:, :],
                                        scalar1=i8f[:, j:j + 1], scalar2=None,
                                        op0=ALU.is_equal))
                v.drain()
                sDV.inc(v.scalar_tensor_tensor(out=tmp32[:, :], in0=onehot[:, :], scalar=1.0,
                                               in1=slot_f[:, tt * NE:(tt + 1) * NE],
                                               op0=ALU.mult, op1=ALU.mult,
                                               accum_out=col2[:, :]))
                v.drain()
                sDV.inc(v.tensor_scalar(out=col3[:, :], in0=col2[:, :], scalar1=float(CAP),
                                        scalar2=None, op0=ALU.is_lt))
                sDV.inc(v.tensor_scalar(out=col1[:, :], in0=i8f[:, j:j + 1],
                                        scalar1=float(CAP), scalar2=None, op0=ALU.mult))
                v.drain()
                sDV.inc(v.tensor_tensor(out=col1[:, :], in0=col1[:, :], in1=col2[:, :],
                                        op=ALU.add))
                v.drain()
                sDV.inc(v.tensor_scalar(out=col2[:, :], in0=col1[:, :],
                                        scalar1=float(NE * CAP), scalar2=None,
                                        op0=ALU.subtract))
                v.drain()
                sDV.inc(v.tensor_tensor(out=col2[:, :], in0=col2[:, :], in1=col3[:, :],
                                        op=ALU.mult))
                v.drain()
                sDV.inc(v.tensor_scalar(out=col2[:, :], in0=col2[:, :],
                                        scalar1=float(NE * CAP), scalar2=None, op0=ALU.add))
                v.drain()
                sDV.inc(v.tensor_scalar(out=gidx_g[:, tt * TOPK + j:tt * TOPK + j + 1],
                                        in0=col2[:, :], scalar1=0.0,
                                        scalar2=None, op0=ALU.add))
                sDV.inc(v.tensor_scalar(out=col4[:, :], in0=col3[:, :], scalar1=-1e7,
                                        scalar2=1e7, op0=ALU.mult, op1=ALU.add))
                v.drain()
                sDV.inc(v.tensor_tensor(out=col4[:, :], in0=col4[:, :], in1=col2[:, :],
                                        op=ALU.add))
                v.drain()
                sDV.inc(v.tensor_scalar(out=gidx_s[:, tt * TOPK + j:tt * TOPK + j + 1],
                                        in0=col4[:, :], scalar1=0.0,
                                        scalar2=None, op0=ALU.add))

        @block.gpsimd
        def _(g, tt=tt, pi=pi):
            _wait(g, sDV)
            for j in range(TOPK):
                sGP.inc(g.indirect_dma_start(
                    out=xg_dram[:, :],
                    out_offset=bass.IndirectOffsetOnAxis(
                        ap=gidx_s[:, tt * TOPK + j:tt * TOPK + j + 1], axis=0),
                    in_=xn_t[pi][:, :], in_offset=None,
                    bounds_check=breg["scat"], oob_is_err=False), 16)
        scat_mark[tt] = sGP.n

    scat_total = sGP.n

    @block.sync
    def _(sy):
        _wait(sy, sAC)
        for kc in range(NKC):
            sST.inc(sy.dma_start(out=xnfm_dram[kc * 128:(kc + 1) * 128, :],
                                 in_=xn_fm[:, kc * T:(kc + 1) * T]), 16)

    if DEBUG:
        @block.sync
        def _(sy):
            _wait(sy, sST)
            _wait(sy, sDV)
            sST.inc(sy.dma_start(out=dbg["dbg_xn"], in_=xn_dram[:, :]), 16)
            for tt in range(NT):
                sST.inc(sy.dma_start(out=dbg["dbg_slot"][tt * 128:(tt + 1) * 128, :],
                                     in_=slot_f[:, tt * NE:(tt + 1) * NE]), 16)
                sST.inc(sy.dma_start(out=dbg["dbg_pj"][tt * 128:(tt + 1) * 128, :],
                                     in_=pj_sb[:, tt * TOPK:(tt + 1) * TOPK]), 16)
                sST.inc(sy.dma_start(out=dbg["dbg_gidx"][tt * 128:(tt + 1) * 128, :],
                                     in_=gidx_g[:, tt * TOPK:(tt + 1) * TOPK]), 16)

    if BUILD_PHASES < 2:
        p1.close()
        return _finish(nc, ctx, block, out_tm, x_in, sLD, sST)

    p2 = ExitStack()
    e2 = p2.enter_context
    wsl = [e2(nc.sbuf_tensor(f"wsl{i}", [128, NKC * 512], BF16)) for i in range(2)]
    stage = [e2(nc.sbuf_tensor(f"stg{i}", [128, 1024], BF16)) for i in range(2)]

    jobs = [(wq_t, q_dram, "fm", AF.Silu), (wk_t, k_dram, "fm", AF.Silu),
            (wv_t, v_dram, "tm", AF.Silu), (wgate_t, gate_dram, "fm", AF.Sigmoid)]
    nslab = len(jobs) * 4
    slab_ready = [None] * nslab
    slab_free = [None] * nslab

    def emit_slab_load(si):
        wdram = jobs[si // 4][0]
        nb = si % 4

        @block.gpsimd
        def _(g, si=si, nb=nb, wdram=wdram):
            if si >= 2:
                _wait(g, sPE, slab_free[si - 2])
            sGP.inc(g.dma_start(
                out=wsl[si % 2].ap().rearrange("p (kc c) -> p kc c", kc=NKC),
                in_=wdram[:, nb * 512:(nb + 1) * 512].rearrange("(kc p) c -> p kc c", p=128)), 16)
        slab_ready[si] = sGP.n

    emit_slab_load(0)
    emit_slab_load(1)

    for si in range(nslab):
        wdram, odram, kind, actf = jobs[si // 4]
        nb = si % 4
        buf = wsl[si % 2]

        @block.tensor
        def _(t, si=si, nb=nb, buf=buf, kind=kind, odram=odram, actf=actf):
            _wait(t, sGP, slab_ready[si])
            if si == 0:
                _wait(t, sAC)
            if kind == "fm":
                for osub in range(4):
                    stg = stage[osub % 2]
                    for th in range(2):
                        ps = psA if th == 0 else psB
                        _wait(t, sAC)
                        for kc in range(NKC):
                            sPE.inc(t.matmul(
                                ps[:, :],
                                buf[:, kc * 512 + osub * 128: kc * 512 + (osub + 1) * 128],
                                xn_fm[:, kc * T + th * 512: kc * T + (th + 1) * 512],
                                start=(kc == 0), stop=(kc == NKC - 1)))

                        @block.scalar
                        def _(sc, osub=osub, th=th, ps=ps, stg=stg, actf=actf):
                            _wait(sc, sPE)
                            _wait(sc, sST)
                            sAC.inc(sc.activation(out=stg[:, th * 512:(th + 1) * 512],
                                                  in_=ps[:, :], func=actf))

                    @block.sync
                    def _(sy, si=si, nb=nb, osub=osub, odram=odram, stg=stg):
                        _wait(sy, sAC)
                        r0 = nb * 512 + osub * 128
                        sST.inc(sy.dma_start(out=odram[r0:r0 + 128, :], in_=stg[:, :]), 16)
            else:
                for ttv in range(NT):
                    ps = psA if ttv % 2 == 0 else psB
                    stg = stage[ttv % 2]
                    _wait(t, sAC)
                    for kc in range(NKC):
                        sPE.inc(t.matmul(
                            ps[:, :],
                            xn_fm[:, kc * T + ttv * 128: kc * T + (ttv + 1) * 128],
                            buf[:, kc * 512:(kc + 1) * 512],
                            start=(kc == 0), stop=(kc == NKC - 1)))

                    @block.scalar
                    def _(sc, ttv=ttv, ps=ps, stg=stg):
                        _wait(sc, sPE)
                        _wait(sc, sST)
                        sAC.inc(sc.activation(out=stg[:, 0:512], in_=ps[:, :], func=AF.Silu))

                    @block.sync
                    def _(sy, ttv=ttv, nb=nb, odram=odram, stg=stg):
                        _wait(sy, sAC)
                        sST.inc(sy.dma_start(
                            out=odram[ttv * 128:(ttv + 1) * 128, nb * 512:(nb + 1) * 512],
                            in_=stg[:, 0:512]), 16)
        slab_free[si] = sPE.n
        if si + 2 < nslab:
            emit_slab_load(si + 2)

    if DEBUG:
        @block.sync
        def _(sy):
            _wait(sy, sST)
            sST.inc(sy.dma_start(out=dbg["dbg_q"], in_=q_dram[:, :]), 16)
            sST.inc(sy.dma_start(out=dbg["dbg_v"], in_=v_dram[:, :]), 16)
            sST.inc(sy.dma_start(out=dbg["dbg_gate"], in_=gate_dram[:, :]), 16)

    phase_barrier(block, ALLSEMS)
    p2.close()
    p1.close()

    if BUILD_PHASES < 3:
        return _finish(nc, ctx, block, out_tm, x_in, sLD, sST)

    p3 = ExitStack()
    e3 = p3.enter_context
    io_m = e3(nc.sbuf_tensor("io_m", [128, 256], F32))
    io_rev = e3(nc.sbuf_tensor("io_rev", [128, 256], F32))
    io_p1 = e3(nc.sbuf_tensor("io_p1", [128, 256], F32))
    e_half = [e3(nc.sbuf_tensor(f"e_half{i}", [128, 256], F32)) for i in range(2)]
    qd_bc = e3(nc.sbuf_tensor("qd_bc", [128, NH * 256], BF16))
    kd_bc = e3(nc.sbuf_tensor("kd_bc", [128, NH * 256], BF16))
    decayT = e3(nc.sbuf_tensor("decayT", [128, NH * 512], BF16))
    U_sb = e3(nc.sbuf_tensor("U_sb", [128, NH * NB * 128], BF16))
    S_a = e3(nc.sbuf_tensor("S_a", [128, NH * 128], BF16))
    S_b = e3(nc.sbuf_tensor("S_b", [128, NH * 128], BF16))
    kv_a = e3(nc.sbuf_tensor("kv_a", [128, NH * 128], BF16))
    kv_b = e3(nc.sbuf_tensor("kv_b", [128, NH * 128], BF16))
    kblk = [e3(nc.sbuf_tensor(f"kblk{i}", [128, 256], BF16)) for i in range(2)]
    vblk = [e3(nc.sbuf_tensor(f"vblk{i}", [128, 256], BF16)) for i in range(2)]
    qblk = [e3(nc.sbuf_tensor(f"qblk{i}", [128, 256], BF16)) for i in range(2)]
    kkd = e3(nc.sbuf_tensor("kkd", [128, 256], BF16))
    kkdT = e3(nc.sbuf_tensor("kkdT", [128, 256], BF16))
    qd_s = e3(nc.sbuf_tensor("qd_s", [128, 256], BF16))
    sdT = e3(nc.sbuf_tensor("sdT", [128, 512], BF16))
    attnb = [e3(nc.sbuf_tensor(f"attnb{i}", [128, 256], BF16)) for i in range(2)]
    tmp_h = e3(nc.sbuf_tensor("tmp_h", [128, 256], F32))
    sgath = e3(nc.sbuf_tensor("sgath", [128, 1024], BF16))

    @block.gpsimd
    def _(g):
        for ap_, pat, base, cm in [
            (io_m[:, :], [[1, 256]], 0, 0),
            (io_rev[:, :], [[-1, 256]], 255, 0),
            (io_p1[:, :], [[1, 256]], 1, 0),
            (e_half[0][:, :], [[1, 256]], 0, -1),
            (e_half[1][:, :], [[1, 256]], -128, -1),
        ]:
            g.iota(ap_, pattern=pat, base=base, channel_multiplier=cm,
                   allow_small_or_imprecise_dtypes=True).then_inc(sGI.h, 1)
            sGI.n += 1

    @block.vector
    def _(v):
        _wait(v, sGI)
        for hf in range(2):
            sDV.inc(v.tensor_scalar(out=tmp_h[:, :], in0=e_half[hf][:, :], scalar1=0.0,
                                    scalar2=None, op0=ALU.is_lt))
            v.drain()
            sDV.inc(v.scalar_tensor_tensor(out=e_half[hf][:, :], in0=tmp_h[:, :],
                                           scalar=1e9, in1=e_half[hf][:, :],
                                           op0=ALU.mult, op1=ALU.add))
        sDV.inc(v.memset(S_a[:, :], 0.0))

    for h in range(NH):
        for dst_ap, io_ap in [
            (qd_bc[:, h * 256:(h + 1) * 256], io_p1),
            (kd_bc[:, h * 256:(h + 1) * 256], io_rev),
            (decayT[:, h * 512: h * 512 + 256], e_half[0]),
            (decayT[:, h * 512 + 256: h * 512 + 512], e_half[1]),
        ]:
            @block.vector
            def _(v, h=h, io_ap=io_ap):
                _wait(v, sAC)
                sDV.inc(v.tensor_scalar(out=tmp_h[:, :], in0=io_ap[:, :],
                                        scalar1=s_bc[:, h:h + 1], scalar2=None, op0=ALU.mult))

            @block.scalar
            def _(sc, dst_ap=dst_ap):
                _wait(sc, sDV)
                sAC.inc(sc.activation(out=dst_ap, in_=tmp_h[:, :], func=AF.Exp, scale=-1.0))

    pe_mark = {}
    for b in range(NB):
        for h in range(NH):
            it = b * NH + h
            pi = it % 2

            @block.sync
            def _(sy, b=b, h=h, pi=pi, it=it):
                if it >= 2:
                    _wait(sy, sPE, pe_mark[it - 2])
                sLD.inc(sy.dma_start(
                    out=kblk[pi][:, :],
                    in_=k_dram[h * 128:(h + 1) * 128, b * 256:(b + 1) * 256]), 16)
                sLD.inc(sy.dma_start(
                    out=vblk[pi].ap().rearrange("p (c e) -> p c e", c=2),
                    in_=v_dram[b * 256:(b + 1) * 256, h * 128:(h + 1) * 128]
                    .rearrange("(c p) e -> p c e", p=128)), 16)

            @block.vector
            def _(v, b=b, h=h, pi=pi):
                _wait(v, sLD)
                _wait(v, sPE)
                sDV.inc(v.tensor_tensor(out=kkd[:, :], in0=kblk[pi][:, :],
                                        in1=kd_bc[:, h * 256:(h + 1) * 256], op=ALU.mult))

            @block.tensor
            def _(t, b=b, h=h, pi=pi):
                _wait(t, sDV)
                _wait(t, sAC)
                for hf in range(2):
                    sPE.inc(t.transpose(psT[:, hf * 128:(hf + 1) * 128],
                                        kkd[:, hf * 128:(hf + 1) * 128], ident_bf[:, :]))

            @block.scalar
            def _(sc, b=b, h=h):
                _wait(sc, sPE)
                sAC.inc(sc.activation(out=kkdT[:, :], in_=psT[:, :], func=AF.Copy))

            @block.tensor
            def _(t, b=b, h=h, pi=pi):
                _wait(t, sAC)
                _wait(t, sDV)
                t.matmul(psU[:, :], kkdT[:, 0:128], vblk[pi][:, 0:128], start=True, stop=False)
                sPE.inc(t.matmul(psU[:, :], kkdT[:, 128:256], vblk[pi][:, 128:256],
                                 start=False, stop=True))
            pe_mark[it] = sPE.n

            @block.scalar
            def _(sc, b=b, h=h):
                _wait(sc, sPE)
                sAC.inc(sc.activation(out=U_sb[:, (h * NB + b) * 128:(h * NB + b + 1) * 128],
                                      in_=psU[:, :], func=AF.Copy))

            @block.vector
            def _(v, b=b, h=h):
                _wait(v, sPE)
                src = S_a if b % 2 == 0 else S_b
                dst = S_b if b % 2 == 0 else S_a
                sDV.inc(v.scalar_tensor_tensor(
                    out=dst[:, h * 128:(h + 1) * 128],
                    in0=src[:, h * 128:(h + 1) * 128], scalar=bd_bc[:, h:h + 1],
                    in1=psU[:, :], op0=ALU.mult, op1=ALU.add))

    @block.sync
    def _(sy):
        _wait(sy, sDV)
        sST.inc(sy.dma_start(
            out=cc_in.ap().rearrange("(h d) e -> d h e", d=128),
            in_=S_a.ap().rearrange("p (h e) -> p h e", h=NH)), 16)

    @block.gpsimd
    def _(g):
        _wait(g, sST)
        sCC.inc(g.collective_compute(
            "AllGather", ALU.bypass, replica_groups=[list(range(NCORES))],
            ins=[cc_in.ap().opt()], outs=[cc_out.ap().opt()]), 1)

    for h in range(NH):
        @block.sync
        def _(sy, h=h):
            if h == 0:
                _wait(sy, sCC)
            else:
                _wait(sy, sDV)
            sLD.inc(sy.dma_start(
                out=sgath.ap().rearrange("p (c e) -> p c e", c=NCORES),
                in_=cc_out.ap().rearrange("(c h d) e -> d c h e", h=NH, d=128)[:, :, h, :]), 16)

        @block.vector
        def _(v, h=h):
            _wait(v, sLD)
            sDV.inc(v.tensor_scalar(out=kv_a[:, h * 128:(h + 1) * 128],
                                    in0=sgath[:, 0:128],
                                    scalar1=w_bcast[:, h * NCORES:h * NCORES + 1],
                                    scalar2=None, op0=ALU.mult))
            v.drain()
            for c in range(1, NCORES):
                dst = kv_b if c % 2 == 1 else kv_a
                src = kv_a if c % 2 == 1 else kv_b
                sDV.inc(v.scalar_tensor_tensor(
                    out=dst[:, h * 128:(h + 1) * 128],
                    in0=sgath[:, c * 128:(c + 1) * 128],
                    scalar=w_bcast[:, h * NCORES + c:h * NCORES + c + 1],
                    in1=src[:, h * 128:(h + 1) * 128], op0=ALU.mult, op1=ALU.add))
                v.drain()

    if BUILD_PHASES < 4:
        p3.close()
        return _finish(nc, ctx, block, out_tm, x_in, sLD, sST)

    pe_mark2 = {}
    for b in range(NB):
        kv_cur = kv_b if b % 2 == 0 else kv_a
        kv_nxt = kv_a if b % 2 == 0 else kv_b
        for h in range(NH):
            it = b * NH + h
            pi = it % 2

            @block.sync
            def _(sy, b=b, h=h, pi=pi, it=it):
                if it >= 2:
                    _wait(sy, sPE, pe_mark2[it - 2])
                elif it == 0:
                    _wait(sy, sPE)
                sLD.inc(sy.dma_start(
                    out=qblk[pi][:, :],
                    in_=q_dram[h * 128:(h + 1) * 128, b * 256:(b + 1) * 256]), 16)
                sLD.inc(sy.dma_start(
                    out=kblk[pi][:, :],
                    in_=k_dram[h * 128:(h + 1) * 128, b * 256:(b + 1) * 256]), 16)
                sLD.inc(sy.dma_start(
                    out=vblk[pi].ap().rearrange("p (c e) -> p c e", c=2),
                    in_=v_dram[b * 256:(b + 1) * 256, h * 128:(h + 1) * 128]
                    .rearrange("(c p) e -> p c e", p=128)), 16)

            @block.vector
            def _(v, b=b, h=h, pi=pi):
                _wait(v, sLD)
                _wait(v, sPE)
                sDV.inc(v.tensor_tensor(out=qd_s[:, :], in0=qblk[pi][:, :],
                                        in1=qd_bc[:, h * 256:(h + 1) * 256], op=ALU.mult))

            @block.tensor
            def _(t, b=b, h=h, pi=pi):
                _wait(t, sDV)
                for hf in range(2):
                    sPE.inc(t.matmul(psC[:, hf * 256:(hf + 1) * 256],
                                     kblk[pi][:, hf * 128:(hf + 1) * 128],
                                     qblk[pi][:, :], start=True, stop=True))

            @block.vector
            def _(v, b=b, h=h):
                _wait(v, sPE)
                for hf in range(2):
                    sDV.inc(v.tensor_tensor(
                        out=sdT[:, hf * 256:(hf + 1) * 256],
                        in0=psC[:, hf * 256:(hf + 1) * 256],
                        in1=decayT[:, h * 512 + hf * 256: h * 512 + (hf + 1) * 256],
                        op=ALU.mult))

            @block.tensor
            def _(t, b=b, h=h, pi=pi, kv_cur=kv_cur):
                _wait(t, sDV)
                _wait(t, sAC)
                t.matmul(psO[:, :], kv_cur[:, h * 128:(h + 1) * 128], qd_s[:, :],
                         start=True, stop=False)
                t.matmul(psO[:, :], vblk[pi][:, 0:128], sdT[:, 0:256],
                         start=False, stop=False)
                sPE.inc(t.matmul(psO[:, :], vblk[pi][:, 128:256], sdT[:, 256:512],
                                 start=False, stop=True))
            pe_mark2[it] = sPE.n

            @block.scalar
            def _(sc, b=b, h=h, it=it):
                _wait(sc, sPE)
                _wait(sc, sST)
                sAC.inc(sc.activation(out=attnb[it % 2][:, :], in_=psO[:, :], func=AF.Copy))

            @block.sync
            def _(sy, b=b, h=h, it=it):
                _wait(sy, sAC)
                sST.inc(sy.dma_start(
                    out=attn_dram[h * 128:(h + 1) * 128, b * 256:(b + 1) * 256],
                    in_=attnb[it % 2][:, :]), 16)

            if b < NB - 1:
                @block.vector
                def _(v, b=b, h=h, kv_cur=kv_cur, kv_nxt=kv_nxt):
                    _wait(v, sPE)
                    sDV.inc(v.scalar_tensor_tensor(
                        out=kv_nxt[:, h * 128:(h + 1) * 128],
                        in0=kv_cur[:, h * 128:(h + 1) * 128],
                        scalar=bd_bc[:, h:h + 1],
                        in1=U_sb[:, (h * NB + b) * 128:(h * NB + b + 1) * 128],
                        op0=ALU.mult, op1=ALU.add))

    if DEBUG:
        @block.sync
        def _(sy):
            _wait(sy, sST)
            sST.inc(sy.dma_start(out=dbg["dbg_attn"], in_=attn_dram[:, :]), 16)

    phase_barrier(block, ALLSEMS)
    p3.close()

    if BUILD_PHASES < 5:
        return _finish(nc, ctx, block, out_tm, x_in, sLD, sST)

    p3b = ExitStack()
    e3b = p3b.enter_context
    sq_s = e3b(nc.sbuf_tensor("sq_s", [128, 512], F32))
    att_ld = [e3b(nc.sbuf_tensor(f"att_ld{i}", [128, 1024], BF16)) for i in range(2)]
    gt_ld = [e3b(nc.sbuf_tensor(f"gt_ld{i}", [128, 1024], BF16)) for i in range(2)]
    gin = e3b(nc.sbuf_tensor("gin", [128, NKC * T], BF16))
    gin_f = e3b(nc.sbuf_tensor("gin_f", [128, 1024], F32))
    wo_sl = [e3b(nc.sbuf_tensor(f"wo_sl{i}", [128, NKC * 512], BF16)) for i in range(2)]
    s_stage = [e3b(nc.sbuf_tensor(f"s_stage{i}", [128, 512], BF16)) for i in range(2)]

    for kc in range(NKC):
        @block.sync
        def _(sy, kc=kc):
            if kc >= 2:
                _wait(sy, sAC)
            sLD.inc(sy.dma_start(out=att_ld[kc % 2][:, :],
                                 in_=attn_dram[kc * 128:(kc + 1) * 128, :]), 16)

        for hf in range(2):
            @block.scalar
            def _(sc, kc=kc, hf=hf):
                _wait(sc, sLD)
                _wait(sc, sPE)
                sAC.inc(sc.activation(out=sq_s[:, :], in_=att_ld[kc % 2][:, hf * 512:(hf + 1) * 512],
                                      func=AF.Square))

            @block.tensor
            def _(t, kc=kc, hf=hf):
                _wait(t, sAC)
                sPE.inc(t.matmul(psD[0:1, 0:512] if hf == 0 else psC[0:1, 0:512],
                                 ones_col_f[:, :], sq_s[:, :],
                                 start=(kc == 0), stop=(kc == NKC - 1)))

    @block.scalar
    def _(sc):
        _wait(sc, sPE)
        sAC.inc(sc.activation(out=var_row[:, 0:512], in_=psD[0:1, 0:512], func=AF.Copy))
        sAC.inc(sc.activation(out=var_row[:, 512:1024], in_=psC[0:1, 0:512], func=AF.Copy))
        sc.drain()
        sAC.inc(sc.activation(out=rstd_row[:, :], in_=var_row[:, :], func=AF.Ln,
                              scale=1.0 / D, bias=epsr_1[:, :]))
        sc.drain()
        sAC.inc(sc.activation(out=rstd_row[:, :], in_=rstd_row[:, :], func=AF.Exp,
                              scale=-0.5))

    @block.tensor
    def _(t):
        _wait(t, sAC)
        t.matmul(psD[:, 0:512], ones_row_f[:, :], rstd_row[:, 0:512], start=True, stop=True)
        sPE.inc(t.matmul(psC[:, 0:512], ones_row_f[:, :], rstd_row[:, 512:1024],
                         start=True, stop=True))

    @block.scalar
    def _(sc):
        _wait(sc, sPE)
        sAC.inc(sc.activation(out=rstd_bc[:, 0:512], in_=psD[:, 0:512], func=AF.Copy))
        sAC.inc(sc.activation(out=rstd_bc[:, 512:1024], in_=psC[:, 0:512], func=AF.Copy))

    for kc in range(NKC):
        @block.sync
        def _(sy, kc=kc):
            if kc >= 2:
                _wait(sy, sDV)
            else:
                _wait(sy, sAC)
            sLD.inc(sy.dma_start(out=att_ld[kc % 2][:, :],
                                 in_=attn_dram[kc * 128:(kc + 1) * 128, :]), 16)
            sLD.inc(sy.dma_start(out=gt_ld[kc % 2][:, :],
                                 in_=gate_dram[kc * 128:(kc + 1) * 128, :]), 16)

        @block.vector
        def _(v, kc=kc):
            _wait(v, sLD)
            _wait(v, sAC)
            sDV.inc(v.scalar_tensor_tensor(out=gin_f[:, :], in0=att_ld[kc % 2][:, :],
                                           scalar=normw_col[:, kc:kc + 1],
                                           in1=rstd_bc[:, :], op0=ALU.mult, op1=ALU.mult))
            v.drain()
            sDV.inc(v.tensor_tensor(out=gin[:, kc * T:(kc + 1) * T], in0=gin_f[:, :],
                                    in1=gt_ld[kc % 2][:, :], op=ALU.mult))

    wo_ready = [None] * 4
    wo_free = [None] * 4

    def emit_wo_load(nb):
        @block.gpsimd
        def _(g, nb=nb):
            if nb >= 2:
                _wait(g, sPE, wo_free[nb - 2])
            sGP.inc(g.dma_start(
                out=wo_sl[nb % 2].ap().rearrange("p (kc c) -> p kc c", kc=NKC),
                in_=wo_t[:, nb * 512:(nb + 1) * 512].rearrange("(kc p) c -> p kc c", p=128)), 16)
        wo_ready[nb] = sGP.n

    emit_wo_load(0)
    emit_wo_load(1)

    for nb in range(4):
        @block.tensor
        def _(t, nb=nb):
            _wait(t, sGP, wo_ready[nb])
            if nb == 0:
                _wait(t, sDV)
            for ttv in range(NT):
                ps = psA if ttv % 2 == 0 else psB
                _wait(t, sAC)
                for kc in range(NKC):
                    sPE.inc(t.matmul(
                        ps[:, :],
                        gin[:, kc * T + ttv * 128: kc * T + (ttv + 1) * 128],
                        wo_sl[nb % 2][:, kc * 512:(kc + 1) * 512],
                        start=(kc == 0), stop=(kc == NKC - 1)))

                @block.scalar
                def _(sc, nb=nb, ttv=ttv, ps=ps):
                    _wait(sc, sPE)
                    _wait(sc, sST)
                    sAC.inc(sc.activation(out=s_stage[ttv % 2][:, :], in_=ps[:, :],
                                          func=AF.Copy))

                @block.sync
                def _(sy, nb=nb, ttv=ttv):
                    _wait(sy, sAC)
                    sST.inc(sy.dma_start(
                        out=s_dram[ttv * 128:(ttv + 1) * 128, nb * 512:(nb + 1) * 512],
                        in_=s_stage[ttv % 2][:, :]), 16)
        wo_free[nb] = sPE.n
        if nb + 2 < 4:
            emit_wo_load(nb + 2)

    if DEBUG:
        @block.sync
        def _(sy):
            _wait(sy, sST)
            sST.inc(sy.dma_start(out=dbg["dbg_s"], in_=s_dram[:, :]), 16)

    phase_barrier(block, ALLSEMS)
    p3b.close()

    if BUILD_PHASES < 6:
        return _finish(nc, ctx, block, out_tm, x_in, sLD, sST)

    p6 = ExitStack()
    e6 = p6.enter_context
    wbufA = e6(nc.sbuf_tensor("wbufA", [128, NKC * NI], BF16))
    wbufB = e6(nc.sbuf_tensor("wbufB", [128, NKC * NI], BF16))
    xg_tm = [e6(nc.sbuf_tensor(f"xg_tm{i}", [128, D], BF16)) for i in range(2)]
    xg_fm = e6(nc.sbuf_tensor("xg_fm", [128, NKC * CAP], BF16))
    g_sb = e6(nc.sbuf_tensor("g_sb", [128, 8 * CAP], BF16))
    u_sb = e6(nc.sbuf_tensor("u_sb", [128, 8 * CAP], BF16))
    dn_sb = e6(nc.sbuf_tensor("dn_sb", [128, 8 * CAP], BF16))
    clip_s = e6(nc.sbuf_tensor("clip_s", [128, CAP], F32))
    y_tm = [e6(nc.sbuf_tensor(f"y_tm{i}", [128, D], F32)) for i in range(2)]

    bufs = [wbufA, wbufB]
    wready = [None] * (3 * NE)
    wfree = [None] * (3 * NE)

    def emit_w_load(si):
        ex = si // 3
        kindw = si % 3
        wd = [eg_t, eu_t, ed_t][kindw]

        @block.gpsimd
        def _(g, si=si, ex=ex, kindw=kindw, wd=wd):
            if si >= 2:
                _wait(g, sPE, wfree[si - 2])
            src = wd[ex]
            kc_n = NKC if kindw < 2 else 8
            sGP.inc(g.dma_start(
                out=bufs[si % 2].ap().rearrange("p (kc c) -> p kc c", kc=kc_n),
                in_=src.rearrange("(kc p) c -> p kc c", p=128)), 16)
        wready[si] = sGP.n

    emit_w_load(0)
    emit_w_load(1)

    for ex in range(NE):
        @block.sync
        def _(sy, ex=ex):
            if ex == 0:
                _wait(sy, sGP, scat_total)
            else:
                _wait(sy, sPE)
            for i in range(2):
                sLD.inc(sy.dma_start(
                    out=xg_tm[i][:, :],
                    in_=xg_dram[ex * CAP + i * 128: ex * CAP + (i + 1) * 128, :]), 16)

        @block.tensor
        def _(t, ex=ex):
            _wait(t, sLD)
            if ex >= 1:
                _wait(t, sDV)
            for kc in range(NKC):
                for i in range(2):
                    idx = kc * 2 + i
                    _wait(t, sAC)
                    sPE.inc(t.transpose(psT[:, (idx % 2) * 128:((idx % 2) + 1) * 128],
                                        xg_tm[i][:, kc * 128:(kc + 1) * 128], ident_bf[:, :]))

                    @block.scalar
                    def _(sc, kc=kc, i=i, idx=idx):
                        _wait(sc, sPE)
                        sAC.inc(sc.activation(
                            out=xg_fm[:, kc * CAP + i * 128: kc * CAP + (i + 1) * 128],
                            in_=psT[:, (idx % 2) * 128:((idx % 2) + 1) * 128], func=AF.Copy))

        for gu in range(2):
            si = ex * 3 + gu
            buf = bufs[si % 2]
            dst = g_sb if gu == 0 else u_sb

            @block.tensor
            def _(t, ex=ex, gu=gu, si=si, buf=buf, dst=dst):
                _wait(t, sGP, wready[si])
                _wait(t, sAC)
                for ic in range(8):
                    ps = psA if ic % 2 == 0 else psB
                    _wait(t, sDV)
                    for kc in range(NKC):
                        sPE.inc(t.matmul(
                            ps[:, 0:CAP],
                            buf[:, kc * NI + ic * 128: kc * NI + (ic + 1) * 128],
                            xg_fm[:, kc * CAP:(kc + 1) * CAP],
                            start=(kc == 0), stop=(kc == NKC - 1)))

                    @block.vector
                    def _(v, ic=ic, ps=ps):
                        _wait(v, sPE)
                        _wait(v, sAC)
                        sDV.inc(v.tensor_scalar(out=clip_s[:, :], in0=ps[:, 0:CAP],
                                                scalar1=ALPHA, scalar2=-ALPHA,
                                                op0=ALU.min, op1=ALU.max))

                    @block.scalar
                    def _(sc, gu=gu, ic=ic, dst=dst):
                        _wait(sc, sDV)
                        sAC.inc(sc.activation(out=dst[:, ic * CAP:(ic + 1) * CAP],
                                              in_=clip_s[:, :],
                                              func=(AF.Silu if gu == 0 else AF.Copy)))
            wfree[si] = sPE.n
            if si + 2 < 3 * NE:
                emit_w_load(si + 2)

        @block.vector
        def _(v, ex=ex):
            _wait(v, sAC)
            for ic in range(8):
                sDV.inc(v.tensor_tensor(out=dn_sb[:, ic * CAP:(ic + 1) * CAP],
                                        in0=g_sb[:, ic * CAP:(ic + 1) * CAP],
                                        in1=u_sb[:, ic * CAP:(ic + 1) * CAP], op=ALU.mult))

        si = ex * 3 + 2
        buf = bufs[si % 2]

        @block.tensor
        def _(t, ex=ex, si=si, buf=buf):
            _wait(t, sGP, wready[si])
            _wait(t, sDV)
            for sc_i in range(2):
                for ncol in range(4):
                    ps = psA if (sc_i * 4 + ncol) % 2 == 0 else psB
                    _wait(t, sAC)
                    for ic in range(8):
                        sPE.inc(t.matmul(
                            ps[:, :],
                            dn_sb[:, ic * CAP + sc_i * 128: ic * CAP + (sc_i + 1) * 128],
                            buf[:, ic * D + ncol * 512: ic * D + (ncol + 1) * 512],
                            start=(ic == 0), stop=(ic == 7)))

                    @block.scalar
                    def _(s2, sc_i=sc_i, ncol=ncol, ps=ps):
                        _wait(s2, sPE)
                        if ncol == 0:
                            _wait(s2, sST)
                        sAC.inc(s2.activation(out=y_tm[sc_i][:, ncol * 512:(ncol + 1) * 512],
                                              in_=ps[:, :], func=AF.Copy))
        wfree[si] = sPE.n
        if si + 2 < 3 * NE:
            emit_w_load(si + 2)

        @block.sync
        def _(sy, ex=ex):
            _wait(sy, sAC)
            for i in range(2):
                sST.inc(sy.dma_start(
                    out=y_dram[ex * CAP + i * 128: ex * CAP + (i + 1) * 128, :],
                    in_=y_tm[i][:, :]), 16)

    if DEBUG:
        @block.sync
        def _(sy):
            _wait(sy, sST)
            sST.inc(sy.dma_start(out=dbg["dbg_y"], in_=y_dram[:, :]), 16)

    phase_barrier(block, ALLSEMS)
    p6.close()

    if BUILD_PHASES < 7:
        return _finish(nc, ctx, block, out_tm, x_in, sLD, sST)

    p7o = ExitStack()
    acc = p7o.enter_context(nc.sbuf_tensor("acc", [128, NT * D], F32))

    p7 = ExitStack()
    e7 = p7.enter_context
    wsh = e7(nc.sbuf_tensor("wsh", [128, NKC * NI], BF16))
    xn_res = e7(nc.sbuf_tensor("xn_res", [128, NKC * T], BF16))
    gsh = e7(nc.sbuf_tensor("gsh", [128, 8 * T], BF16))
    ush = e7(nc.sbuf_tensor("ush", [128, 8 * T], BF16))
    dnsh = e7(nc.sbuf_tensor("dnsh", [128, 8 * T], BF16))
    clip_sh = e7(nc.sbuf_tensor("clip_sh", [128, 512], F32))

    @block.sync
    def _(sy):
        for kc in range(NKC):
            sLD.inc(sy.dma_start(out=xn_res[:, kc * T:(kc + 1) * T],
                                 in_=xnfm_dram[kc * 128:(kc + 1) * 128, :]), 16)

    sh_ready = {}
    for gu in range(2):
        wd = sg_t if gu == 0 else su_t
        dst = gsh if gu == 0 else ush

        @block.gpsimd
        def _(g, gu=gu, wd=wd):
            if gu > 0:
                _wait(g, sPE)
            sGP.inc(g.dma_start(
                out=wsh.ap().rearrange("p (kc c) -> p kc c", kc=NKC),
                in_=wd.rearrange("(kc p) c -> p kc c", p=128)), 16)
        sh_ready[gu] = sGP.n

        for ic in range(8):
            for th in range(2):
                ps = psA if (ic * 2 + th) % 2 == 0 else psB

                @block.tensor
                def _(t, gu=gu, ic=ic, th=th, ps=ps):
                    _wait(t, sGP, sh_ready[gu])
                    _wait(t, sLD)
                    _wait(t, sDV)
                    for kc in range(NKC):
                        sPE.inc(t.matmul(
                            ps[:, :],
                            wsh[:, kc * NI + ic * 128: kc * NI + (ic + 1) * 128],
                            xn_res[:, kc * T + th * 512: kc * T + (th + 1) * 512],
                            start=(kc == 0), stop=(kc == NKC - 1)))

                @block.vector
                def _(v, gu=gu, ic=ic, th=th, ps=ps):
                    _wait(v, sPE)
                    _wait(v, sAC)
                    sDV.inc(v.tensor_scalar(out=clip_sh[:, :], in0=ps[:, :],
                                            scalar1=ALPHA, scalar2=-ALPHA,
                                            op0=ALU.min, op1=ALU.max))

                @block.scalar
                def _(sc, gu=gu, ic=ic, th=th, dst=dst):
                    _wait(sc, sDV)
                    sAC.inc(sc.activation(
                        out=dst[:, ic * T + th * 512: ic * T + (th + 1) * 512],
                        in_=clip_sh[:, :], func=(AF.Silu if gu == 0 else AF.Copy)))

    @block.vector
    def _(v):
        _wait(v, sAC)
        for ic in range(8):
            sDV.inc(v.tensor_tensor(out=dnsh[:, ic * T:(ic + 1) * T],
                                    in0=gsh[:, ic * T:(ic + 1) * T],
                                    in1=ush[:, ic * T:(ic + 1) * T], op=ALU.mult))

    @block.gpsimd
    def _(g):
        _wait(g, sDV)
        sGP.inc(g.dma_start(
            out=wsh.ap().rearrange("p (kc c) -> p kc c", kc=8),
            in_=sd_t.rearrange("(kc p) c -> p kc c", p=128)), 16)
    sh_ready[2] = sGP.n

    for ttv in range(NT):
        for ncol in range(4):
            ps = psA if (ttv * 4 + ncol) % 2 == 0 else psB

            @block.tensor
            def _(t, ttv=ttv, ncol=ncol, ps=ps):
                _wait(t, sGP, sh_ready[2])
                _wait(t, sAC)
                for ic in range(8):
                    sPE.inc(t.matmul(
                        ps[:, :],
                        dnsh[:, ic * T + ttv * 128: ic * T + (ttv + 1) * 128],
                        wsh[:, ic * D + ncol * 512: ic * D + (ncol + 1) * 512],
                        start=(ic == 0), stop=(ic == 7)))

            @block.scalar
            def _(sc, ttv=ttv, ncol=ncol, ps=ps):
                _wait(sc, sPE)
                sAC.inc(sc.activation(
                    out=acc[:, ttv * D + ncol * 512: ttv * D + (ncol + 1) * 512],
                    in_=ps[:, :], func=AF.Copy))

    phase_barrier(block, ALLSEMS)
    p7.close()

    p8 = ExitStack()
    ybuf = [p8.enter_context(nc.sbuf_tensor(f"ybuf{i}", [128, D], F32)) for i in range(2)]
    acc_t1 = p8.enter_context(nc.sbuf_tensor("acc_t1", [128, D], F32))
    acc_done = sAC.n

    for tt in range(NT):
        for j in range(TOPK):
            it = tt * TOPK + j

            @block.gpsimd
            def _(g, tt=tt, j=j, it=it):
                if it == 0:
                    _wait(g, sST)
                if it >= 2:
                    _wait(g, sDV)
                sGP.inc(g.indirect_dma_start(
                    out=ybuf[it % 2][:, :], out_offset=None,
                    in_=y_dram[:, :],
                    in_offset=bass.IndirectOffsetOnAxis(
                        ap=gidx_g[:, tt * TOPK + j: tt * TOPK + j + 1], axis=0),
                    bounds_check=breg["gath"], oob_is_err=False), 16)

            @block.vector
            def _(v, tt=tt, j=j, it=it):
                _wait(v, sGP)
                if it == 0:
                    _wait(v, sAC, acc_done)
                dst = acc_t1[:, :] if j % 2 == 0 else acc[:, tt * D:(tt + 1) * D]
                srcb = acc[:, tt * D:(tt + 1) * D] if j % 2 == 0 else acc_t1[:, :]
                sDV.inc(v.scalar_tensor_tensor(
                    out=dst,
                    in0=ybuf[it % 2][:, :],
                    scalar=pj_sb[:, tt * TOPK + j: tt * TOPK + j + 1],
                    in1=srcb, op0=ALU.mult, op1=ALU.add))
                v.drain()

    phase_barrier(block, ALLSEMS)
    p8.close()

    p9 = ExitStack()
    e9 = p9.enter_context
    g_f = e9(nc.sbuf_tensor("g_f", [128, D], F32))
    prod9 = e9(nc.sbuf_tensor("prod9", [128, D], F32))
    x_ld = [e9(nc.sbuf_tensor(f"x_ld{i}", [128, D], F32)) for i in range(2)]
    s_ld = [e9(nc.sbuf_tensor(f"s_ld{i}", [128, D], BF16)) for i in range(2)]
    o_st = [e9(nc.sbuf_tensor(f"o_st{i}", [128, D], F32)) for i in range(2)]

    for tt in range(NT):
        @block.sync
        def _(sy, tt=tt):
            if tt >= 2:
                _wait(sy, sDV)
            sLD.inc(sy.dma_start(out=x_ld[tt % 2][:, :],
                                 in_=x_in[tt * 128:(tt + 1) * 128, :]), 16)
            sLD.inc(sy.dma_start(out=s_ld[tt % 2][:, :],
                                 in_=s_dram[tt * 128:(tt + 1) * 128, :]), 16)

        @block.scalar
        def _(sc, tt=tt):
            _wait(sc, sDV)
            if tt >= 1:
                _wait(sc, sAC)
            sAC.inc(sc.activation(out=g_f[:, :], in_=acc[:, tt * D:(tt + 1) * D],
                                  func=AF.Silu))

        @block.vector
        def _(v, tt=tt):
            _wait(v, sAC)
            _wait(v, sLD)
            _wait(v, sST)
            sDV.inc(v.tensor_tensor(out=prod9[:, :], in0=g_f[:, :], in1=s_ld[tt % 2][:, :],
                                    op=ALU.mult))
            v.drain()
            sDV.inc(v.tensor_tensor(out=o_st[tt % 2][:, :], in0=prod9[:, :],
                                    in1=x_ld[tt % 2][:, :], op=ALU.add))

        @block.sync
        def _(sy, tt=tt):
            _wait(sy, sDV)
            sST.inc(sy.dma_start(out=out_tm[tt * 128:(tt + 1) * 128, :],
                                 in_=o_st[tt % 2][:, :]), 16)

    @block.sync
    def _(sy):
        _wait(sy, sST)

    p9.close()
    p7o.close()
    ctx.close()
    return nc


def _finish(nc, ctx, block, out_tm, x_in, sLD, sST):
    with nc.sbuf_tensor("fin_buf", [128, D], F32) as buf:
        @block.sync
        def _(sy):
            _wait(sy, sST)
            _wait(sy, sLD)
            for tt in range(NT):
                sLD.inc(sy.dma_start(out=buf[:, :], in_=x_in[tt * 128:(tt + 1) * 128, :]), 16)
                _wait(sy, sLD)
                sST.inc(sy.dma_start(out=out_tm[tt * 128:(tt + 1) * 128, :], in_=buf[:, :]), 16)
            _wait(sy, sST)
    ctx.close()
    return nc


_CACHE = {}


def _get_nc():
    key = (BUILD_PHASES, DEBUG)
    if key not in _CACHE:
        _CACHE[key] = build_nc()
    return _CACHE[key]


def host_prep(inputs):
    x = np.asarray(inputs["L0_attn_hidden_states"])[0]
    slope = np.ascontiguousarray(np.asarray(inputs["L0_attn_slope_rate"]).reshape(NH, 1))
    wqkv = np.asarray(inputs["L0_attn_w_qkv"]).reshape(NH, 3, HD, D)
    wq_t = np.ascontiguousarray(wqkv[:, 0].reshape(NH * HD, D).T)
    wk_t = np.ascontiguousarray(wqkv[:, 1].reshape(NH * HD, D).T)
    wv_t = np.ascontiguousarray(wqkv[:, 2].reshape(NH * HD, D).T)
    wgate_t = np.ascontiguousarray(np.asarray(inputs["L0_attn_w_gate"]).T)
    wo_t = np.ascontiguousarray(np.asarray(inputs["L0_attn_w_out"]).T)
    normw = np.ascontiguousarray(np.asarray(inputs["L0_attn_norm_w"]).reshape(NKC, 128))
    router_t = np.ascontiguousarray(np.asarray(inputs["L0_ffn_router_w"]).T)
    ebias = np.ascontiguousarray(np.asarray(inputs["L0_ffn_expert_bias"]).reshape(1, NE))
    eg_t = np.ascontiguousarray(np.asarray(inputs["L0_ffn_gate_w"]).transpose(0, 2, 1))
    eu_t = np.ascontiguousarray(np.asarray(inputs["L0_ffn_up_w"]).transpose(0, 2, 1))
    ed_t = np.ascontiguousarray(np.asarray(inputs["L0_ffn_down_w"]).transpose(0, 2, 1))
    sg_t = np.ascontiguousarray(np.asarray(inputs["L0_ffn_shared_gate_w"]).T)
    su_t = np.ascontiguousarray(np.asarray(inputs["L0_ffn_shared_up_w"]).T)
    sd_t = np.ascontiguousarray(np.asarray(inputs["L0_ffn_shared_down_w"]).T)

    in_maps = []
    for c in range(NCORES):
        ec = np.full((NH, NCORES), 1e9, np.float32)
        for cp in range(c):
            ec[:, cp] = 1024.0 * (c - 1 - cp)
        in_maps.append({
            "x": np.ascontiguousarray(x[c * T:(c + 1) * T]),
            "slope": slope, "wq_t": wq_t, "wk_t": wk_t, "wv_t": wv_t,
            "wgate_t": wgate_t, "wo_t": wo_t, "normw": normw,
            "router_t": router_t, "ebias": ebias,
            "eg_t": eg_t, "eu_t": eu_t, "ed_t": ed_t,
            "sg_t": sg_t, "su_t": su_t, "sd_t": sd_t,
            "e_const": ec,
        })
    return in_maps


def kernel(**inputs):
    nc = _get_nc()
    in_maps = host_prep(inputs)
    res = run_bass_kernel_spmd(nc, in_maps, core_ids=list(range(NCORES)))
    out = np.concatenate([r["out_tm"] for r in res.results], axis=0)
    if DEBUG:
        kernel.last_results = res
    return out.reshape(1, NCORES * T, D).astype(np.float32)


# version 02
# speedup vs baseline: 1.1506x; 1.1506x over previous
import numpy as np
from contextlib import ExitStack

import concourse.bass as bass
import concourse.mybir as mybir
from concourse.bass_utils import run_bass_kernel_spmd

F32 = mybir.dt.float32
BF16 = mybir.dt.bfloat16
U32 = mybir.dt.uint32
AF = mybir.ActivationFunctionType
ALU = mybir.AluOpType

NCORES = 8
T = 1024
D = 2048
NH = 16
HD = 128
BLK = 256
NB = T // BLK
NI = 1024
NE = 32
TOPK = 4
CAP = 256
ALPHA = 7.0
EPS_LN = 1e-5
EPS_RMS = 1e-6
NT = T // 128
NKC = D // 128

BUILD_PHASES = 99
DEBUG = False


class Sem:
    def __init__(self, ctx, nc, name):
        self.h = ctx.enter_context(nc.semaphore(name))
        self.n = 0

    def inc(self, instr, k=1):
        instr.then_inc(self.h, k)
        self.n += k
        return self.n


def _wait(eng, sem, val=None):
    v = sem.n if val is None else val
    if v > 0:
        eng.wait_ge(sem.h, v)


def phase_barrier(block, sems):
    for mk in (block.sync, block.gpsimd, block.vector, block.scalar, block.tensor):
        @mk
        def _(eng):
            for s in sems:
                if s.n > 0:
                    eng.wait_ge(s.h, s.n)


def build_nc():
    nc = bass.Bass("TRN2", target_bir_lowering=False, debug=False)

    def din(name, shape, dt=F32):
        return nc.dram_tensor(name, shape, dt, kind="ExternalInput").ap()

    x_in = din("x", [T, D])
    slope = din("slope", [NH, 1])
    wq_t = din("wq_t", [D, D])
    wk_t = din("wk_t", [D, D])
    wv_t = din("wv_t", [D, D])
    wgate_t = din("wgate_t", [D, D])
    wo_t = din("wo_t", [D, D])
    normw = din("normw", [NKC, 128])
    router_t = din("router_t", [D, NE])
    ebias = din("ebias", [1, NE])
    eg_t = din("eg_t", [NE, D, NI])
    eu_t = din("eu_t", [NE, D, NI])
    ed_t = din("ed_t", [NE, NI, D])
    sg_t = din("sg_t", [D, NI])
    su_t = din("su_t", [D, NI])
    sd_t = din("sd_t", [NI, D])
    e_const = din("e_const", [NH, NCORES])

    out_tm = nc.dram_tensor("out_tm", [T, D], F32, kind="ExternalOutput").ap()

    xn_dram = nc.dram_tensor("xn_dram", [T, D], BF16, kind="Internal").ap()
    xnfm_dram = nc.dram_tensor("xnfm_dram", [D, T], BF16, kind="Internal").ap()
    q_dram = nc.dram_tensor("q_dram", [D, T], BF16, kind="Internal").ap()
    k_dram = nc.dram_tensor("k_dram", [D, T], BF16, kind="Internal").ap()
    v_dram = nc.dram_tensor("v_dram", [T, D], BF16, kind="Internal").ap()
    gate_dram = nc.dram_tensor("gate_dram", [D, T], BF16, kind="Internal").ap()
    attn_dram = nc.dram_tensor("attn_dram", [D, T], BF16, kind="Internal").ap()
    s_dram = nc.dram_tensor("s_dram", [T, D], BF16, kind="Internal").ap()
    xg_dram = nc.dram_tensor("xg_dram", [NE * CAP, D], BF16, kind="Internal").ap()
    y_dram = nc.dram_tensor("y_dram", [NE * CAP + 1, D], F32, kind="Internal").ap()
    w_dram = nc.dram_tensor("w_dram", [NH, NCORES], F32, kind="Internal").ap()
    qdc_dram = nc.dram_tensor("qdc_dram", [NH, 256], F32, kind="Internal").ap()
    kdc_dram = nc.dram_tensor("kdc_dram", [NH, 256], F32, kind="Internal").ap()
    cc_in = nc.dram_tensor("cc_in", [NH * HD, HD], BF16, kind="Internal")
    cc_out = nc.dram_tensor("cc_out", [NCORES * NH * HD, HD], BF16, kind="Internal")

    dbg = {}
    if DEBUG:
        for name, shape, dt in [
            ("dbg_xn", [T, D], BF16), ("dbg_slot", [T, NE], F32),
            ("dbg_q", [D, T], BF16), ("dbg_v", [T, D], BF16),
            ("dbg_attn", [D, T], BF16), ("dbg_s", [T, D], BF16),
            ("dbg_y", [NE * CAP + 1, D], F32), ("dbg_gate", [D, T], BF16),
            ("dbg_pj", [T, TOPK], F32), ("dbg_gidx", [T, TOPK], U32),
        ]:
            dbg[name] = nc.dram_tensor(name, shape, dt, kind="ExternalOutput").ap()

    ctx = ExitStack()
    e = ctx.enter_context

    sLD = Sem(ctx, nc, "sLD")
    sGP = Sem(ctx, nc, "sGP")
    sPE = Sem(ctx, nc, "sPE")
    sDV = Sem(ctx, nc, "sDV")
    sAC = Sem(ctx, nc, "sAC")
    sST = Sem(ctx, nc, "sST")
    sCC = Sem(ctx, nc, "sCC")
    sGI = Sem(ctx, nc, "sGI")
    ALLSEMS = [sLD, sGP, sPE, sDV, sAC, sST, sCC, sGI]

    sb = lambda name, shape, dt: e(nc.sbuf_tensor(name, shape, dt))
    ident_bf = sb("ident_bf", [128, 128], BF16)
    ident_f = sb("ident_f", [128, 128], F32)
    upper_bf = sb("upper_bf", [128, 128], BF16)
    ones_row_f = sb("ones_row_f", [1, 128], F32)
    ones_col_b = sb("ones_col_b", [128, 1], BF16)
    ones_col_f = sb("ones_col_f", [128, 1], F32)
    iota32 = sb("iota32", [128, NE], F32)
    io_row = sb("io_row", [128, 128], F32)
    io_col = sb("io_col", [128, 1], F32)
    normw_col = sb("normw_col", [128, NKC], F32)
    bias_bc = sb("bias_bc", [128, NE], F32)
    s_col = sb("s_col", [NH, 1], F32)
    s_bc = sb("s_bc", [128, NH], F32)
    bd_bc = sb("bd_bc", [128, NH], F32)
    w_bcast = sb("w_bcast", [128, 128], F32)
    slot_f = sb("slot_f", [128, NT * NE], F32)
    pj_sb = sb("pj_sb", [128, NT * TOPK], F32)
    gidx_g = sb("gidx_g", [128, NT * TOPK], U32)
    gidx_s = sb("gidx_s", [128, NT * TOPK], U32)
    var_row = sb("var_row", [1, T], F32)
    rstd_row = sb("rstd_row", [1, T], F32)
    rstd_bc = sb("rstd_bc", [128, T], BF16)
    zrow = sb("zrow", [1, D], F32)
    small = sb("small", [NH, NCORES], F32)
    wrow = sb("wrow", [1, 128], F32)
    sprow = sb("sprow", [1, NH], F32)
    bdrow = sb("bdrow", [1, NH], F32)
    nw16 = sb("nw16", [NKC, 128], F32)
    eps_col = sb("eps_col", [128, 1], F32)
    epsr_1 = sb("epsr_1", [1, 1], F32)

    psA = e(nc.psum_tensor("psA", [128, 512], F32))
    psB = e(nc.psum_tensor("psB", [128, 512], F32))
    psC = e(nc.psum_tensor("psC", [128, 512], F32))
    psD = e(nc.psum_tensor("psD", [128, 512], F32))
    psT = e(nc.psum_tensor("psT", [128, 256], BF16))
    psT2 = e(nc.psum_tensor("psT2", [128, 256], BF16))
    psU = e(nc.psum_tensor("psU", [128, 256], F32))
    psO = e(nc.psum_tensor("psO", [128, 256], F32))

    block = e(nc.Block())

    @block.gpsimd
    def _(g):
        for ap_, pat, base, cm in [
            (iota32[:, :], [[1, NE]], 0, 0),
            (io_row[:, :], [[1, 128]], 0, 0),
            (io_col[:, :], [[0, 1]], 0, 1),
        ]:
            g.iota(ap_, pattern=pat, base=base, channel_multiplier=cm,
                   allow_small_or_imprecise_dtypes=True).then_inc(sGI.h, 1)
            sGI.n += 1

    @block.sync
    def _(sy):
        sLD.inc(sy.dma_start(out=s_col[:, :], in_=slope[:, :]), 16)
        sLD.inc(sy.dma_start(out=nw16[:, :], in_=normw[:, :]), 16)
        sLD.inc(sy.dma_start(out=bias_bc[0:1, :], in_=ebias[:, :]), 16)
        sLD.inc(sy.dma_start(out=small[:, :], in_=e_const[:, :]), 16)

    @block.vector
    def _(v):
        _wait(v, sGI)
        sDV.inc(v.tensor_scalar(out=ident_bf[:, :], in0=io_row[:, :], scalar1=io_col[:, :],
                                scalar2=None, op0=ALU.is_equal))
        sDV.inc(v.tensor_scalar(out=ident_f[:, :], in0=io_row[:, :], scalar1=io_col[:, :],
                                scalar2=None, op0=ALU.is_equal))
        sDV.inc(v.tensor_scalar(out=upper_bf[:, :], in0=io_row[:, :], scalar1=io_col[:, :],
                                scalar2=None, op0=ALU.is_gt))
        sDV.inc(v.memset(ones_row_f[:, :], 1.0))
        sDV.inc(v.memset(ones_col_b[:, :], 1.0))
        sDV.inc(v.memset(ones_col_f[:, :], 1.0))
        sDV.inc(v.memset(zrow[:, :], 0.0))
        sDV.inc(v.memset(eps_col[:, :], EPS_LN))
        sDV.inc(v.memset(epsr_1[:, :], EPS_RMS))
        _wait(v, sLD, 16 * 4)
        sDV.inc(v.tensor_scalar(out=small[:, :], in0=small[:, :], scalar1=s_col[:, :],
                                scalar2=None, op0=ALU.mult))

    @block.scalar
    def _(sc):
        _wait(sc, sDV)
        sAC.inc(sc.activation(out=small[:, :], in_=small[:, :], func=AF.Exp, scale=-1.0))

    @block.sync
    def _(sy):
        _wait(sy, sAC)
        sST.inc(sy.dma_start(out=w_dram[:, :], in_=small[:, :]), 16)
        _wait(sy, sST)
        sLD.inc(sy.dma_start(
            out=wrow[:, :],
            in_=w_dram.rearrange("(one a) b -> one (a b)", one=1)), 16)
        sST.inc(sy.dma_start(out=y_dram[NE * CAP:NE * CAP + 1, :], in_=zrow[:, :]), 16)

    @block.tensor
    def _(t):
        _wait(t, sDV)
        _wait(t, sLD)
        t.transpose(psC[:, 0:NKC], nw16[:, :], ident_f[0:NKC, 0:NKC])
        sPE.inc(t.transpose(psD[0:1, 0:NH], s_col[:, :], ident_f[0:NH, 0:NH]))

    @block.scalar
    def _(sc):
        _wait(sc, sPE)
        sAC.inc(sc.activation(out=normw_col[:, :], in_=psC[:, 0:NKC], func=AF.Copy))
        sAC.inc(sc.activation(out=sprow[:, :], in_=psD[0:1, 0:NH], func=AF.Copy))
        sc.drain()
        sAC.inc(sc.activation(out=bdrow[:, :], in_=sprow[:, :], func=AF.Exp, scale=-256.0))

    @block.tensor
    def _(t):
        _wait(t, sAC)
        _wait(t, sLD)
        t.matmul(psC[:, 0:NH], ones_row_f[:, :], sprow[:, :], start=True, stop=True)
        t.matmul(psD[:, 0:NH], ones_row_f[:, :], bdrow[:, :], start=True, stop=True)
        t.matmul(psA[:, 0:NE], ones_row_f[:, :], bias_bc[0:1, :], start=True, stop=True)
        sPE.inc(t.matmul(psB[:, 0:128], ones_row_f[:, :], wrow[:, :], start=True, stop=True))

    @block.scalar
    def _(sc):
        _wait(sc, sPE)
        sAC.inc(sc.activation(out=s_bc[:, :], in_=psC[:, 0:NH], func=AF.Copy))
        sAC.inc(sc.activation(out=bd_bc[:, :], in_=psD[:, 0:NH], func=AF.Copy))
        sAC.inc(sc.activation(out=w_bcast[:, :], in_=psB[:, 0:128], func=AF.Copy))

    @block.vector
    def _(v):
        _wait(v, sPE)
        sDV.inc(v.tensor_copy(bias_bc[:, :], psA[:, 0:NE]))

    if BUILD_PHASES < 1:
        return _finish(nc, ctx, block, out_tm, x_in, sLD, sST)

    scat_mark = {}
    p1 = ExitStack()
    e1 = p1.enter_context
    x_t = [e1(nc.sbuf_tensor(f"x_t{i}", [128, D], F32)) for i in range(2)]
    xn_t = [e1(nc.sbuf_tensor(f"xn_t{i}", [128, D], BF16)) for i in range(2)]
    junk_b = e1(nc.sbuf_tensor("junk_b", [128, D], BF16))
    junk_f = e1(nc.sbuf_tensor("junk_f", [128, D], F32))
    sum_c = e1(nc.sbuf_tensor("sum_c", [128, 1], F32))
    ssq_c = e1(nc.sbuf_tensor("ssq_c", [128, 1], F32))
    mean_c = e1(nc.sbuf_tensor("mean_c", [128, 1], F32))
    rstd_c = e1(nc.sbuf_tensor("rstd_c", [128, 1], F32))
    xn_fm = e1(nc.sbuf_tensor("xn_fm", [128, NKC * T], BF16))
    rwt_sb = e1(nc.sbuf_tensor("rwt_sb", [128, NKC * NE], BF16))
    logit = e1(nc.sbuf_tensor("logit", [128, NE], F32))
    sc_abs = e1(nc.sbuf_tensor("sc_abs", [128, NE], F32))
    sc_sp = e1(nc.sbuf_tensor("sc_sp", [128, NE], F32))
    scores = e1(nc.sbuf_tensor("scores", [128, NE], F32))
    biased = e1(nc.sbuf_tensor("biased", [128, NE], F32))
    v8 = e1(nc.sbuf_tensor("v8", [128, 8], F32))
    i8 = e1(nc.sbuf_tensor("i8", [128, 8], U32))
    i8f = e1(nc.sbuf_tensor("i8f", [128, 8], F32))
    onehot = e1(nc.sbuf_tensor("onehot", [128, NE], F32))
    topv = e1(nc.sbuf_tensor("topv", [128, TOPK], F32))
    denom = e1(nc.sbuf_tensor("denom", [128, 1], F32))
    mk_b = e1(nc.sbuf_tensor("mk_b", [128, NE], BF16))
    mk_f = e1(nc.sbuf_tensor("mk_f", [128, NE], F32))
    pfx = e1(nc.sbuf_tensor("pfx", [1, NE], F32))
    tmp32 = e1(nc.sbuf_tensor("tmp32", [128, NE], F32))
    col1 = e1(nc.sbuf_tensor("col1", [128, 1], F32))
    col2 = e1(nc.sbuf_tensor("col2", [128, 1], F32))
    col3 = e1(nc.sbuf_tensor("col3", [128, 1], F32))
    col4 = e1(nc.sbuf_tensor("col4", [128, 1], F32))

    breg = {}

    @block.gpsimd
    def _(g):
        breg["scat"] = g.to_reg(NE * CAP - 1)
        breg["gath"] = g.to_reg(NE * CAP)
        sGP.inc(g.dma_start(out=rwt_sb.ap().rearrange("p (kc c) -> p kc c", kc=NKC),
                            in_=router_t.rearrange("(kc p) c -> p kc c", p=128)), 16)
    rwt_ready = sGP.n

    @block.vector
    def _(v):
        sDV.inc(v.memset(pfx[:, :], 0.0))

    for tt in range(NT):
        pi = tt % 2

        @block.sync
        def _(sy, tt=tt, pi=pi):
            if tt >= 2:
                _wait(sy, sAC)
                _wait(sy, sDV)
            sLD.inc(sy.dma_start(out=x_t[pi][:, :], in_=x_in[tt * 128:(tt + 1) * 128, :]), 16)

        @block.scalar
        def _(sc, tt=tt, pi=pi):
            _wait(sc, sLD)
            if tt >= 1:
                _wait(sc, sDV)
            sAC.inc(sc.activation(out=junk_b[:, :], in_=x_t[pi][:, :], func=AF.Copy,
                                  accum_out=sum_c[:, :]))
            sAC.inc(sc.activation(out=junk_b[:, :], in_=x_t[pi][:, :], func=AF.Square,
                                  accum_out=ssq_c[:, :]))

        @block.vector
        def _(v, tt=tt, pi=pi):
            _wait(v, sLD)
            if tt >= 2:
                v.wait_ge(sGP.h, scat_mark[tt - 2])
            _wait(v, sAC)
            sDV.inc(v.tensor_scalar(out=mean_c[:, :], in0=sum_c[:, :], scalar1=1.0 / D,
                                    scalar2=None, op0=ALU.mult))
            sDV.inc(v.tensor_scalar(out=col1[:, :], in0=ssq_c[:, :], scalar1=1.0 / D,
                                    scalar2=None, op0=ALU.mult))
            v.drain()
            sDV.inc(v.tensor_tensor(out=col2[:, :], in0=mean_c[:, :], in1=mean_c[:, :],
                                    op=ALU.mult))
            v.drain()
            sDV.inc(v.tensor_tensor(out=col3[:, :], in0=col1[:, :], in1=col2[:, :],
                                    op=ALU.subtract))
            v.drain()

        @block.scalar
        def _(sc, tt=tt):
            _wait(sc, sDV)
            sAC.inc(sc.activation(out=col4[:, :], in_=col3[:, :], func=AF.Sqrt, bias=eps_col[:, :]))

        @block.vector
        def _(v, tt=tt, pi=pi):
            _wait(v, sAC)
            sDV.inc(v.reciprocal(rstd_c[:, :], col4[:, :]))
            v.drain()
            sDV.inc(v.tensor_scalar(out=xn_t[pi][:, :], in0=x_t[pi][:, :],
                                    scalar1=mean_c[:, :], scalar2=rstd_c[:, :],
                                    op0=ALU.subtract, op1=ALU.mult))

        @block.sync
        def _(sy, tt=tt, pi=pi):
            _wait(sy, sDV)
            sST.inc(sy.dma_start(out=xn_dram[tt * 128:(tt + 1) * 128, :], in_=xn_t[pi][:, :]), 16)

        @block.tensor
        def _(t, tt=tt, pi=pi):
            _wait(t, sDV)
            trm = {}
            for kc in range(NKC):
                pst = psT if kc % 2 == 0 else psT2
                if kc >= 2:
                    _wait(t, sAC, trm[kc - 2])
                else:
                    _wait(t, sAC)
                sPE.inc(t.transpose(pst[:, 0:128],
                                    xn_t[pi][:, kc * 128:(kc + 1) * 128], ident_bf[:, :]))
                pev = sPE.n

                @block.scalar
                def _(sc, tt=tt, kc=kc, pst=pst, pev=pev):
                    _wait(sc, sPE, pev)
                    sAC.inc(sc.activation(
                        out=xn_fm[:, kc * T + tt * 128: kc * T + (tt + 1) * 128],
                        in_=pst[:, 0:128], func=AF.Copy))
                trm[kc] = sAC.n

        @block.tensor
        def _(t, tt=tt):
            _wait(t, sAC)
            _wait(t, sDV)
            _wait(t, sGP, rwt_ready)
            for kc in range(NKC):
                sPE.inc(t.matmul(psO[:, 0:NE],
                                 xn_fm[:, kc * T + tt * 128: kc * T + (tt + 1) * 128],
                                 rwt_sb[:, kc * NE:(kc + 1) * NE],
                                 start=(kc == 0), stop=(kc == NKC - 1)))

        @block.vector
        def _(v, tt=tt):
            _wait(v, sPE)
            sDV.inc(v.tensor_copy(logit[:, :], psO[:, 0:NE]))

        @block.scalar
        def _(sc, tt=tt):
            _wait(sc, sDV)
            sAC.inc(sc.activation(out=sc_abs[:, :], in_=logit[:, :], func=AF.Abs))
            sc.drain()
            sAC.inc(sc.activation(out=sc_sp[:, :], in_=sc_abs[:, :], func=AF.Exp, scale=-1.0))
            sc.drain()
            sAC.inc(sc.activation(out=sc_sp[:, :], in_=sc_sp[:, :], func=AF.Ln, bias=1.0))

        @block.vector
        def _(v, tt=tt):
            _wait(v, sAC)
            sDV.inc(v.tensor_scalar(out=sc_abs[:, :], in0=logit[:, :], scalar1=0.0,
                                    scalar2=None, op0=ALU.max))
            v.drain()
            sDV.inc(v.tensor_tensor(out=sc_sp[:, :], in0=sc_sp[:, :], in1=sc_abs[:, :],
                                    op=ALU.add))

        @block.scalar
        def _(sc, tt=tt):
            _wait(sc, sDV)
            sAC.inc(sc.activation(out=scores[:, :], in_=sc_sp[:, :], func=AF.Sqrt))

        @block.vector
        def _(v, tt=tt):
            _wait(v, sAC)
            sDV.inc(v.tensor_tensor(out=biased[:, :], in0=scores[:, :], in1=bias_bc[:, :],
                                    op=ALU.add))
            v.drain()
            v.max(v8[:, :], biased[:, :])
            v.drain()
            sDV.inc(v.max_index(i8[:, :], v8[:, :], biased[:, :]))
            v.drain()
            sDV.inc(v.tensor_scalar(out=i8f[:, :], in0=i8[:, :], scalar1=0.0,
                                    scalar2=None, op0=ALU.add))
            sDV.inc(v.tensor_scalar(out=mk_f[:, :], in0=biased[:, :], scalar1=v8[:, 3:4],
                                    scalar2=None, op0=ALU.is_ge))
            v.drain()
            sDV.inc(v.tensor_copy(mk_b[:, :], mk_f[:, :]))
            for j in range(TOPK):
                sDV.inc(v.tensor_scalar(out=onehot[:, :], in0=iota32[:, :],
                                        scalar1=i8f[:, j:j + 1], scalar2=None,
                                        op0=ALU.is_equal))
                v.drain()
                sDV.inc(v.scalar_tensor_tensor(out=tmp32[:, :], in0=onehot[:, :], scalar=1.0,
                                               in1=scores[:, :], op0=ALU.mult, op1=ALU.mult,
                                               accum_out=topv[:, j:j + 1]))
                v.drain()
            sDV.inc(v.tensor_reduce(out=denom[:, :], in_=topv[:, 0:TOPK],
                                    axis=mybir.AxisListType.X, op=ALU.add))
            v.drain()
            sDV.inc(v.tensor_scalar(out=denom[:, :], in0=denom[:, :], scalar1=1e-20,
                                    scalar2=None, op0=ALU.add))
            v.drain()
            sDV.inc(v.reciprocal(col1[:, :], denom[:, :]))
            v.drain()
            for j in range(TOPK):
                sDV.inc(v.tensor_scalar(out=pj_sb[:, tt * TOPK + j:tt * TOPK + j + 1],
                                        in0=topv[:, j:j + 1], scalar1=col1[:, :],
                                        scalar2=None, op0=ALU.mult))

        @block.tensor
        def _(t, tt=tt):
            _wait(t, sDV)
            t.matmul(psO[:, 0:NE], upper_bf[:, :], mk_b[:, :], start=True, stop=False)
            sPE.inc(t.matmul(psO[:, 0:NE], ones_row_f[:, :], pfx[:, :], start=False, stop=True))
            sPE.inc(t.matmul(psD[0:1, 0:NE], ones_col_b[:, :], mk_b[:, :], start=True, stop=True))

        @block.vector
        def _(v, tt=tt):
            _wait(v, sPE)
            sDV.inc(v.tensor_scalar(out=tmp32[:, :], in0=mk_f[:, :], scalar1=-1e9,
                                    scalar2=1e9, op0=ALU.mult, op1=ALU.add))
            v.drain()
            sDV.inc(v.tensor_tensor(out=slot_f[:, tt * NE:(tt + 1) * NE], in0=psO[:, 0:NE],
                                    in1=tmp32[:, :], op=ALU.add))
            sDV.inc(v.tensor_tensor(out=pfx[:, :], in0=pfx[:, :], in1=psD[0:1, 0:NE],
                                    op=ALU.add))
            v.drain()
            for j in range(TOPK):
                sDV.inc(v.tensor_scalar(out=onehot[:, :], in0=iota32[:, :],
                                        scalar1=i8f[:, j:j + 1], scalar2=None,
                                        op0=ALU.is_equal))
                v.drain()
                sDV.inc(v.scalar_tensor_tensor(out=tmp32[:, :], in0=onehot[:, :], scalar=1.0,
                                               in1=slot_f[:, tt * NE:(tt + 1) * NE],
                                               op0=ALU.mult, op1=ALU.mult,
                                               accum_out=col2[:, :]))
                v.drain()
                sDV.inc(v.tensor_scalar(out=col3[:, :], in0=col2[:, :], scalar1=float(CAP),
                                        scalar2=None, op0=ALU.is_lt))
                sDV.inc(v.tensor_scalar(out=col1[:, :], in0=i8f[:, j:j + 1],
                                        scalar1=float(CAP), scalar2=None, op0=ALU.mult))
                v.drain()
                sDV.inc(v.tensor_tensor(out=col1[:, :], in0=col1[:, :], in1=col2[:, :],
                                        op=ALU.add))
                v.drain()
                sDV.inc(v.tensor_scalar(out=col2[:, :], in0=col1[:, :],
                                        scalar1=float(NE * CAP), scalar2=None,
                                        op0=ALU.subtract))
                v.drain()
                sDV.inc(v.tensor_tensor(out=col2[:, :], in0=col2[:, :], in1=col3[:, :],
                                        op=ALU.mult))
                v.drain()
                sDV.inc(v.tensor_scalar(out=col2[:, :], in0=col2[:, :],
                                        scalar1=float(NE * CAP), scalar2=None, op0=ALU.add))
                v.drain()
                sDV.inc(v.tensor_scalar(out=gidx_g[:, tt * TOPK + j:tt * TOPK + j + 1],
                                        in0=col2[:, :], scalar1=0.0,
                                        scalar2=None, op0=ALU.add))
                sDV.inc(v.tensor_scalar(out=col4[:, :], in0=col3[:, :], scalar1=-1e7,
                                        scalar2=1e7, op0=ALU.mult, op1=ALU.add))
                v.drain()
                sDV.inc(v.tensor_tensor(out=col4[:, :], in0=col4[:, :], in1=col2[:, :],
                                        op=ALU.add))
                v.drain()
                sDV.inc(v.tensor_scalar(out=gidx_s[:, tt * TOPK + j:tt * TOPK + j + 1],
                                        in0=col4[:, :], scalar1=0.0,
                                        scalar2=None, op0=ALU.add))

        @block.gpsimd
        def _(g, tt=tt, pi=pi):
            _wait(g, sDV)
            for j in range(TOPK):
                sGP.inc(g.indirect_dma_start(
                    out=xg_dram[:, :],
                    out_offset=bass.IndirectOffsetOnAxis(
                        ap=gidx_s[:, tt * TOPK + j:tt * TOPK + j + 1], axis=0),
                    in_=xn_t[pi][:, :], in_offset=None,
                    bounds_check=breg["scat"], oob_is_err=False), 16)
        scat_mark[tt] = sGP.n

    scat_total = sGP.n

    @block.sync
    def _(sy):
        _wait(sy, sAC)
        for kc in range(NKC):
            sST.inc(sy.dma_start(out=xnfm_dram[kc * 128:(kc + 1) * 128, :],
                                 in_=xn_fm[:, kc * T:(kc + 1) * T]), 16)

    if DEBUG:
        @block.sync
        def _(sy):
            _wait(sy, sST)
            _wait(sy, sDV)
            sST.inc(sy.dma_start(out=dbg["dbg_xn"], in_=xn_dram[:, :]), 16)
            for tt in range(NT):
                sST.inc(sy.dma_start(out=dbg["dbg_slot"][tt * 128:(tt + 1) * 128, :],
                                     in_=slot_f[:, tt * NE:(tt + 1) * NE]), 16)
                sST.inc(sy.dma_start(out=dbg["dbg_pj"][tt * 128:(tt + 1) * 128, :],
                                     in_=pj_sb[:, tt * TOPK:(tt + 1) * TOPK]), 16)
                sST.inc(sy.dma_start(out=dbg["dbg_gidx"][tt * 128:(tt + 1) * 128, :],
                                     in_=gidx_g[:, tt * TOPK:(tt + 1) * TOPK]), 16)

    if BUILD_PHASES < 2:
        p1.close()
        return _finish(nc, ctx, block, out_tm, x_in, sLD, sST)

    p2 = ExitStack()
    e2 = p2.enter_context
    wsl = [e2(nc.sbuf_tensor(f"wsl{i}", [128, NKC * 512], BF16)) for i in range(2)]
    stage = [e2(nc.sbuf_tensor(f"stg{i}", [128, 1024], BF16)) for i in range(2)]

    jobs = [(wq_t, q_dram, "fm", AF.Silu), (wk_t, k_dram, "fm", AF.Silu),
            (wv_t, v_dram, "tm", AF.Silu), (wgate_t, gate_dram, "fm", AF.Sigmoid)]
    nslab = len(jobs) * 4
    slab_ready = [None] * nslab
    slab_free = [None] * nslab

    def emit_slab_load(si):
        wdram = jobs[si // 4][0]
        nb = si % 4

        @block.gpsimd
        def _(g, si=si, nb=nb, wdram=wdram):
            if si >= 2:
                _wait(g, sPE, slab_free[si - 2])
            sGP.inc(g.dma_start(
                out=wsl[si % 2].ap().rearrange("p (kc c) -> p kc c", kc=NKC),
                in_=wdram[:, nb * 512:(nb + 1) * 512].rearrange("(kc p) c -> p kc c", p=128)), 16)
        slab_ready[si] = sGP.n

    emit_slab_load(0)
    emit_slab_load(1)

    for si in range(nslab):
        wdram, odram, kind, actf = jobs[si // 4]
        nb = si % 4
        buf = wsl[si % 2]

        @block.tensor
        def _(t, si=si, nb=nb, buf=buf, kind=kind, odram=odram, actf=actf):
            _wait(t, sGP, slab_ready[si])
            if si == 0:
                _wait(t, sAC)
            if kind == "fm":
                for osub in range(4):
                    stg = stage[osub % 2]
                    for th in range(2):
                        ps = psA if th == 0 else psB
                        _wait(t, sAC)
                        for kc in range(NKC):
                            sPE.inc(t.matmul(
                                ps[:, :],
                                buf[:, kc * 512 + osub * 128: kc * 512 + (osub + 1) * 128],
                                xn_fm[:, kc * T + th * 512: kc * T + (th + 1) * 512],
                                start=(kc == 0), stop=(kc == NKC - 1)))

                        @block.scalar
                        def _(sc, osub=osub, th=th, ps=ps, stg=stg, actf=actf):
                            _wait(sc, sPE)
                            _wait(sc, sST)
                            sAC.inc(sc.activation(out=stg[:, th * 512:(th + 1) * 512],
                                                  in_=ps[:, :], func=actf))

                    @block.sync
                    def _(sy, si=si, nb=nb, osub=osub, odram=odram, stg=stg):
                        _wait(sy, sAC)
                        r0 = nb * 512 + osub * 128
                        sST.inc(sy.dma_start(out=odram[r0:r0 + 128, :], in_=stg[:, :]), 16)
            else:
                for ttv in range(NT):
                    ps = psA if ttv % 2 == 0 else psB
                    stg = stage[ttv % 2]
                    _wait(t, sAC)
                    for kc in range(NKC):
                        sPE.inc(t.matmul(
                            ps[:, :],
                            xn_fm[:, kc * T + ttv * 128: kc * T + (ttv + 1) * 128],
                            buf[:, kc * 512:(kc + 1) * 512],
                            start=(kc == 0), stop=(kc == NKC - 1)))

                    @block.scalar
                    def _(sc, ttv=ttv, ps=ps, stg=stg):
                        _wait(sc, sPE)
                        _wait(sc, sST)
                        sAC.inc(sc.activation(out=stg[:, 0:512], in_=ps[:, :], func=AF.Silu))

                    @block.sync
                    def _(sy, ttv=ttv, nb=nb, odram=odram, stg=stg):
                        _wait(sy, sAC)
                        sST.inc(sy.dma_start(
                            out=odram[ttv * 128:(ttv + 1) * 128, nb * 512:(nb + 1) * 512],
                            in_=stg[:, 0:512]), 16)
        slab_free[si] = sPE.n
        if si + 2 < nslab:
            emit_slab_load(si + 2)

    if DEBUG:
        @block.sync
        def _(sy):
            _wait(sy, sST)
            sST.inc(sy.dma_start(out=dbg["dbg_q"], in_=q_dram[:, :]), 16)
            sST.inc(sy.dma_start(out=dbg["dbg_v"], in_=v_dram[:, :]), 16)
            sST.inc(sy.dma_start(out=dbg["dbg_gate"], in_=gate_dram[:, :]), 16)

    phase_barrier(block, ALLSEMS)
    p2.close()
    p1.close()

    if BUILD_PHASES < 3:
        return _finish(nc, ctx, block, out_tm, x_in, sLD, sST)

    p3 = ExitStack()
    e3 = p3.enter_context
    io_m = e3(nc.sbuf_tensor("io_m", [128, 256], F32))
    io_rev = e3(nc.sbuf_tensor("io_rev", [128, 256], F32))
    io_p1 = e3(nc.sbuf_tensor("io_p1", [128, 256], F32))
    e_half = [e3(nc.sbuf_tensor(f"e_half{i}", [128, 256], F32)) for i in range(2)]
    qd_bc = e3(nc.sbuf_tensor("qd_bc", [128, NH * 256], BF16))
    kd_bc = e3(nc.sbuf_tensor("kd_bc", [128, NH * 256], BF16))
    decayT = e3(nc.sbuf_tensor("decayT", [128, NH * 512], BF16))
    U_sb = e3(nc.sbuf_tensor("U_sb", [128, NH * NB * 128], BF16))
    S_a = e3(nc.sbuf_tensor("S_a", [128, NH * 128], BF16))
    S_b = e3(nc.sbuf_tensor("S_b", [128, NH * 128], BF16))
    kv_a = e3(nc.sbuf_tensor("kv_a", [128, NH * 128], BF16))
    kv_b = e3(nc.sbuf_tensor("kv_b", [128, NH * 128], BF16))
    kblk = [e3(nc.sbuf_tensor(f"kblk{i}", [128, 256], BF16)) for i in range(2)]
    vblk = [e3(nc.sbuf_tensor(f"vblk{i}", [128, 256], BF16)) for i in range(2)]
    qblk = [e3(nc.sbuf_tensor(f"qblk{i}", [128, 256], BF16)) for i in range(2)]
    kkd = e3(nc.sbuf_tensor("kkd", [128, 256], BF16))
    kkdT = e3(nc.sbuf_tensor("kkdT", [128, 256], BF16))
    qd_s = e3(nc.sbuf_tensor("qd_s", [128, 256], BF16))
    sdT = e3(nc.sbuf_tensor("sdT", [128, 512], BF16))
    attnb = [e3(nc.sbuf_tensor(f"attnb{i}", [128, 256], BF16)) for i in range(2)]
    tmp_h = e3(nc.sbuf_tensor("tmp_h", [128, 256], F32))
    sgath = e3(nc.sbuf_tensor("sgath", [128, 1024], BF16))

    @block.gpsimd
    def _(g):
        for ap_, pat, base, cm in [
            (io_m[:, :], [[1, 256]], 0, 0),
            (io_rev[:, :], [[-1, 256]], 255, 0),
            (io_p1[:, :], [[1, 256]], 1, 0),
            (e_half[0][:, :], [[1, 256]], 0, -1),
            (e_half[1][:, :], [[1, 256]], -128, -1),
        ]:
            g.iota(ap_, pattern=pat, base=base, channel_multiplier=cm,
                   allow_small_or_imprecise_dtypes=True).then_inc(sGI.h, 1)
            sGI.n += 1

    @block.vector
    def _(v):
        _wait(v, sGI)
        for hf in range(2):
            sDV.inc(v.tensor_scalar(out=tmp_h[:, :], in0=e_half[hf][:, :], scalar1=0.0,
                                    scalar2=None, op0=ALU.is_lt))
            v.drain()
            sDV.inc(v.scalar_tensor_tensor(out=e_half[hf][:, :], in0=tmp_h[:, :],
                                           scalar=1e9, in1=e_half[hf][:, :],
                                           op0=ALU.mult, op1=ALU.add))
        sDV.inc(v.memset(S_a[:, :], 0.0))

    qdc = e3(nc.sbuf_tensor("qdc", [NH, 256], F32))
    kdc = e3(nc.sbuf_tensor("kdc", [NH, 256], F32))
    qdc_row = e3(nc.sbuf_tensor("qdc_row", [1, NH * 256], F32))
    kdc_row = e3(nc.sbuf_tensor("kdc_row", [1, NH * 256], F32))
    tmp_h2 = e3(nc.sbuf_tensor("tmp_h2", [128, 256], F32))

    @block.vector
    def _(v):
        _wait(v, sDV)
        sDV.inc(v.tensor_scalar(out=qdc[:, :], in0=io_p1[0:NH, :], scalar1=s_col[:, :],
                                scalar2=None, op0=ALU.mult))
        sDV.inc(v.tensor_scalar(out=kdc[:, :], in0=io_rev[0:NH, :], scalar1=s_col[:, :],
                                scalar2=None, op0=ALU.mult))

    @block.scalar
    def _(sc):
        _wait(sc, sDV)
        sAC.inc(sc.activation(out=qdc[:, :], in_=qdc[:, :], func=AF.Exp, scale=-1.0))
        sc.drain()
        sAC.inc(sc.activation(out=kdc[:, :], in_=kdc[:, :], func=AF.Exp, scale=-1.0))
        sc.drain()
    qdc_ac = sAC.n

    @block.sync
    def _(sy):
        _wait(sy, sAC, qdc_ac)
        sST.inc(sy.dma_start(out=qdc_dram[:, :], in_=qdc[:, :]), 16)
        sST.inc(sy.dma_start(out=kdc_dram[:, :], in_=kdc[:, :]), 16)
        _wait(sy, sST)
        sLD.inc(sy.dma_start(out=qdc_row[:, :],
                             in_=qdc_dram.rearrange("(one a) b -> one (a b)", one=1)), 16)
        sLD.inc(sy.dma_start(out=kdc_row[:, :],
                             in_=kdc_dram.rearrange("(one a) b -> one (a b)", one=1)), 16)
    qdc_ld = sLD.n

    bcm = {}
    for h in range(NH):
        for w2, (cm, dst) in enumerate([(qdc_row, qd_bc), (kdc_row, kd_bc)]):
            g = h * 2 + w2
            ps = psU if g % 2 == 0 else psO

            @block.tensor
            def _(t, h=h, g=g, cm=cm, ps=ps):
                if g == 0:
                    _wait(t, sLD, qdc_ld)
                if g >= 2:
                    _wait(t, sAC, bcm[g - 2])
                sPE.inc(t.matmul(ps[:, :], ones_row_f[:, :],
                                 cm[0:1, h * 256:(h + 1) * 256],
                                 start=True, stop=True))
                pev = sPE.n

                @block.scalar
                def _(sc, h=h, g=g, ps=ps, dst=dst, pev=pev):
                    _wait(sc, sPE, pev)
                    sAC.inc(sc.activation(out=dst[:, h * 256:(h + 1) * 256],
                                          in_=ps[:, :], func=AF.Copy))
                bcm[g] = sAC.n

    dtm_dv = {}
    dtm_ac = {}
    for h in range(NH):
        for hf in range(2):
            g = h * 2 + hf
            tb = tmp_h if g % 2 == 0 else tmp_h2

            @block.vector
            def _(v, h=h, hf=hf, g=g, tb=tb):
                if g >= 2:
                    _wait(v, sAC, dtm_ac[g - 2])
                sDV.inc(v.tensor_scalar(out=tb[:, :], in0=e_half[hf][:, :],
                                        scalar1=s_bc[:, h:h + 1], scalar2=None, op0=ALU.mult))
                dtm_dv[g] = sDV.n

            @block.scalar
            def _(sc, h=h, hf=hf, g=g, tb=tb):
                _wait(sc, sDV, dtm_dv[g])
                sAC.inc(sc.activation(out=decayT[:, h * 512 + hf * 256: h * 512 + (hf + 1) * 256],
                                      in_=tb[:, :], func=AF.Exp, scale=-1.0))
                dtm_ac[g] = sAC.n

    pe_mark = {}
    u_ld = {}
    u_kkd_dv = {}
    u_tr_pe = {}
    u_kkdT_ac = {}
    u_mm_pe = {}
    u_ev_ac = {}
    u_s_dv = {}
    kkd2 = e3(nc.sbuf_tensor("kkd2", [128, 256], BF16))
    kkdT2 = e3(nc.sbuf_tensor("kkdT2", [128, 256], BF16))
    for b in range(NB):
        for h in range(NH):
            it = b * NH + h
            pi = it % 2
            kkd_b = kkd if pi == 0 else kkd2
            kkdT_b = kkdT if pi == 0 else kkdT2
            psu_b = psU if pi == 0 else psO

            @block.sync
            def _(sy, b=b, h=h, pi=pi, it=it):
                if it >= 2:
                    _wait(sy, sPE, u_mm_pe[it - 2])
                sLD.inc(sy.dma_start(
                    out=kblk[pi][:, :],
                    in_=k_dram[h * 128:(h + 1) * 128, b * 256:(b + 1) * 256]), 16)
                sLD.inc(sy.dma_start(
                    out=vblk[pi].ap().rearrange("p (c e) -> p c e", c=2),
                    in_=v_dram[b * 256:(b + 1) * 256, h * 128:(h + 1) * 128]
                    .rearrange("(c p) e -> p c e", p=128)), 16)
            u_ld[it] = sLD.n

            @block.vector
            def _(v, b=b, h=h, pi=pi, it=it, kkd_b=kkd_b):
                _wait(v, sLD, u_ld[it])
                if it >= 2:
                    _wait(v, sPE, u_tr_pe[it - 2])
                sDV.inc(v.tensor_tensor(out=kkd_b[:, :], in0=kblk[pi][:, :],
                                        in1=kd_bc[:, h * 256:(h + 1) * 256], op=ALU.mult))
                u_kkd_dv[it] = sDV.n

            @block.tensor
            def _(t, b=b, h=h, pi=pi, it=it, kkd_b=kkd_b):
                _wait(t, sDV, u_kkd_dv[it])
                if it >= 2:
                    _wait(t, sAC, u_kkdT_ac[it - 2])
                pst = psT if pi == 0 else psT2
                for hf in range(2):
                    sPE.inc(t.transpose(pst[:, hf * 128:(hf + 1) * 128],
                                        kkd_b[:, hf * 128:(hf + 1) * 128], ident_bf[:, :]))
                u_tr_pe[it] = sPE.n

            @block.scalar
            def _(sc, b=b, h=h, pi=pi, it=it, kkdT_b=kkdT_b):
                _wait(sc, sPE, u_tr_pe[it])
                pst = psT if pi == 0 else psT2
                sAC.inc(sc.activation(out=kkdT_b[:, :], in_=pst[:, :], func=AF.Copy))
                u_kkdT_ac[it] = sAC.n

            @block.tensor
            def _(t, b=b, h=h, pi=pi, it=it, kkdT_b=kkdT_b, psu_b=psu_b):
                _wait(t, sAC, u_kkdT_ac[it])
                if it >= 2:
                    _wait(t, sDV, u_s_dv[it - 2])
                    _wait(t, sAC, u_ev_ac[it - 2])
                t.matmul(psu_b[:, 0:128], kkdT_b[:, 0:128], vblk[pi][:, 0:128],
                         start=True, stop=False)
                sPE.inc(t.matmul(psu_b[:, 0:128], kkdT_b[:, 128:256], vblk[pi][:, 128:256],
                                 start=False, stop=True))
                u_mm_pe[it] = sPE.n
            pe_mark[it] = sPE.n

            @block.scalar
            def _(sc, b=b, h=h, it=it, psu_b=psu_b):
                _wait(sc, sPE, u_mm_pe[it])
                sAC.inc(sc.activation(out=U_sb[:, (h * NB + b) * 128:(h * NB + b + 1) * 128],
                                      in_=psu_b[:, 0:128], func=AF.Copy))
                u_ev_ac[it] = sAC.n

            @block.vector
            def _(v, b=b, h=h, it=it, psu_b=psu_b):
                _wait(v, sPE, u_mm_pe[it])
                src_ = S_a if b % 2 == 0 else S_b
                dst = S_b if b % 2 == 0 else S_a
                sDV.inc(v.scalar_tensor_tensor(
                    out=dst[:, h * 128:(h + 1) * 128],
                    in0=src_[:, h * 128:(h + 1) * 128], scalar=bd_bc[:, h:h + 1],
                    in1=psu_b[:, 0:128], op0=ALU.mult, op1=ALU.add))
                u_s_dv[it] = sDV.n

    @block.sync
    def _(sy):
        _wait(sy, sDV)
        sST.inc(sy.dma_start(
            out=cc_in.ap().rearrange("(h d) e -> d h e", d=128),
            in_=S_a.ap().rearrange("p (h e) -> p h e", h=NH)), 16)

    @block.gpsimd
    def _(g):
        _wait(g, sST)
        sCC.inc(g.collective_compute(
            "AllGather", ALU.bypass, replica_groups=[list(range(NCORES))],
            ins=[cc_in.ap().opt()], outs=[cc_out.ap().opt()]), 1)

    for h in range(NH):
        @block.sync
        def _(sy, h=h):
            if h == 0:
                _wait(sy, sCC)
            else:
                _wait(sy, sDV)
            sLD.inc(sy.dma_start(
                out=sgath.ap().rearrange("p (c e) -> p c e", c=NCORES),
                in_=cc_out.ap().rearrange("(c h d) e -> d c h e", h=NH, d=128)[:, :, h, :]), 16)

        @block.vector
        def _(v, h=h):
            _wait(v, sLD)
            sDV.inc(v.tensor_scalar(out=kv_a[:, h * 128:(h + 1) * 128],
                                    in0=sgath[:, 0:128],
                                    scalar1=w_bcast[:, h * NCORES:h * NCORES + 1],
                                    scalar2=None, op0=ALU.mult))
            v.drain()
            for c in range(1, NCORES):
                dst = kv_b if c % 2 == 1 else kv_a
                src = kv_a if c % 2 == 1 else kv_b
                sDV.inc(v.scalar_tensor_tensor(
                    out=dst[:, h * 128:(h + 1) * 128],
                    in0=sgath[:, c * 128:(c + 1) * 128],
                    scalar=w_bcast[:, h * NCORES + c:h * NCORES + c + 1],
                    in1=src[:, h * 128:(h + 1) * 128], op0=ALU.mult, op1=ALU.add))
                v.drain()

    if BUILD_PHASES < 4:
        p3.close()
        return _finish(nc, ctx, block, out_tm, x_in, sLD, sST)

    qd_s2 = e3(nc.sbuf_tensor("qd_s2", [128, 256], BF16))
    sdT2 = e3(nc.sbuf_tensor("sdT2", [128, 512], BF16))
    a_ld = {}
    a_qd_dv = {}
    a_sc_pe = {}
    a_sdT_dv = {}
    a_o_pe = {}
    a_ev_ac = {}
    a_st = {}
    a_kv_dv = {}
    for b in range(NB):
        kv_cur = kv_b if b % 2 == 0 else kv_a
        kv_nxt = kv_a if b % 2 == 0 else kv_b
        for h in range(NH):
            it = b * NH + h
            pi = it % 2
            qd_b = qd_s if pi == 0 else qd_s2
            sdT_b = sdT if pi == 0 else sdT2
            pssc = psC if pi == 0 else psD
            pso = psO if pi == 0 else psU

            @block.sync
            def _(sy, b=b, h=h, pi=pi, it=it):
                if it >= 2:
                    _wait(sy, sPE, a_o_pe[it - 2])
                elif it == 0:
                    _wait(sy, sPE)
                sLD.inc(sy.dma_start(
                    out=qblk[pi][:, :],
                    in_=q_dram[h * 128:(h + 1) * 128, b * 256:(b + 1) * 256]), 16)
                sLD.inc(sy.dma_start(
                    out=kblk[pi][:, :],
                    in_=k_dram[h * 128:(h + 1) * 128, b * 256:(b + 1) * 256]), 16)
                sLD.inc(sy.dma_start(
                    out=vblk[pi].ap().rearrange("p (c e) -> p c e", c=2),
                    in_=v_dram[b * 256:(b + 1) * 256, h * 128:(h + 1) * 128]
                    .rearrange("(c p) e -> p c e", p=128)), 16)
            a_ld[it] = sLD.n

            @block.vector
            def _(v, b=b, h=h, pi=pi, it=it, qd_b=qd_b):
                _wait(v, sLD, a_ld[it])
                if it >= 2:
                    _wait(v, sPE, a_o_pe[it - 2])
                sDV.inc(v.tensor_tensor(out=qd_b[:, :], in0=qblk[pi][:, :],
                                        in1=qd_bc[:, h * 256:(h + 1) * 256], op=ALU.mult))
                a_qd_dv[it] = sDV.n

            @block.tensor
            def _(t, b=b, h=h, pi=pi, it=it, pssc=pssc):
                _wait(t, sLD, a_ld[it])
                if it >= 2:
                    _wait(t, sDV, a_sdT_dv[it - 2])
                for hf in range(2):
                    sPE.inc(t.matmul(pssc[:, hf * 256:(hf + 1) * 256],
                                     kblk[pi][:, hf * 128:(hf + 1) * 128],
                                     qblk[pi][:, :], start=True, stop=True))
                a_sc_pe[it] = sPE.n

            @block.vector
            def _(v, b=b, h=h, it=it, pssc=pssc, sdT_b=sdT_b):
                _wait(v, sPE, a_sc_pe[it])
                if it >= 2:
                    _wait(v, sPE, a_o_pe[it - 2])
                for hf in range(2):
                    sDV.inc(v.tensor_tensor(
                        out=sdT_b[:, hf * 256:(hf + 1) * 256],
                        in0=pssc[:, hf * 256:(hf + 1) * 256],
                        in1=decayT[:, h * 512 + hf * 256: h * 512 + (hf + 1) * 256],
                        op=ALU.mult))
                a_sdT_dv[it] = sDV.n

            @block.tensor
            def _(t, b=b, h=h, pi=pi, it=it, kv_cur=kv_cur, qd_b=qd_b, sdT_b=sdT_b, pso=pso):
                _wait(t, sDV, a_sdT_dv[it])
                _wait(t, sDV, a_qd_dv[it])
                if b > 0:
                    _wait(t, sDV, a_kv_dv[(b - 1) * NH + h])
                if it >= 2:
                    _wait(t, sAC, a_ev_ac[it - 2])
                else:
                    _wait(t, sAC, u_ev_ac[NB * NH - 1])
                t.matmul(pso[:, :], kv_cur[:, h * 128:(h + 1) * 128], qd_b[:, :],
                         start=True, stop=False)
                t.matmul(pso[:, :], vblk[pi][:, 0:128], sdT_b[:, 0:256],
                         start=False, stop=False)
                sPE.inc(t.matmul(pso[:, :], vblk[pi][:, 128:256], sdT_b[:, 256:512],
                                 start=False, stop=True))
                a_o_pe[it] = sPE.n

            @block.scalar
            def _(sc, b=b, h=h, it=it, pso=pso):
                _wait(sc, sPE, a_o_pe[it])
                if it >= 2:
                    _wait(sc, sST, a_st[it - 2])
                sAC.inc(sc.activation(out=attnb[it % 2][:, :], in_=pso[:, :], func=AF.Copy))
                a_ev_ac[it] = sAC.n

            @block.sync
            def _(sy, b=b, h=h, it=it):
                _wait(sy, sAC, a_ev_ac[it])
                sST.inc(sy.dma_start(
                    out=attn_dram[h * 128:(h + 1) * 128, b * 256:(b + 1) * 256],
                    in_=attnb[it % 2][:, :]), 16)
            a_st[it] = sST.n

            if b < NB - 1:
                @block.vector
                def _(v, b=b, h=h, it=it, kv_cur=kv_cur, kv_nxt=kv_nxt):
                    _wait(v, sPE, a_o_pe[it])
                    sDV.inc(v.scalar_tensor_tensor(
                        out=kv_nxt[:, h * 128:(h + 1) * 128],
                        in0=kv_cur[:, h * 128:(h + 1) * 128],
                        scalar=bd_bc[:, h:h + 1],
                        in1=U_sb[:, (h * NB + b) * 128:(h * NB + b + 1) * 128],
                        op0=ALU.mult, op1=ALU.add))
                    a_kv_dv[it] = sDV.n

    if DEBUG:
        @block.sync
        def _(sy):
            _wait(sy, sST)
            sST.inc(sy.dma_start(out=dbg["dbg_attn"], in_=attn_dram[:, :]), 16)

    phase_barrier(block, ALLSEMS)
    p3.close()

    if BUILD_PHASES < 5:
        return _finish(nc, ctx, block, out_tm, x_in, sLD, sST)

    p3b = ExitStack()
    e3b = p3b.enter_context
    sq_s = e3b(nc.sbuf_tensor("sq_s", [128, 512], F32))
    att_ld = [e3b(nc.sbuf_tensor(f"att_ld{i}", [128, 1024], BF16)) for i in range(2)]
    gt_ld = [e3b(nc.sbuf_tensor(f"gt_ld{i}", [128, 1024], BF16)) for i in range(2)]
    gin = e3b(nc.sbuf_tensor("gin", [128, NKC * T], BF16))
    gin_f = e3b(nc.sbuf_tensor("gin_f", [128, 1024], F32))
    wo_sl = [e3b(nc.sbuf_tensor(f"wo_sl{i}", [128, NKC * 512], BF16)) for i in range(2)]
    s_stage = [e3b(nc.sbuf_tensor(f"s_stage{i}", [128, 512], BF16)) for i in range(2)]

    for kc in range(NKC):
        @block.sync
        def _(sy, kc=kc):
            if kc >= 2:
                _wait(sy, sAC)
            sLD.inc(sy.dma_start(out=att_ld[kc % 2][:, :],
                                 in_=attn_dram[kc * 128:(kc + 1) * 128, :]), 16)

        for hf in range(2):
            @block.scalar
            def _(sc, kc=kc, hf=hf):
                _wait(sc, sLD)
                _wait(sc, sPE)
                sAC.inc(sc.activation(out=sq_s[:, :], in_=att_ld[kc % 2][:, hf * 512:(hf + 1) * 512],
                                      func=AF.Square))

            @block.tensor
            def _(t, kc=kc, hf=hf):
                _wait(t, sAC)
                sPE.inc(t.matmul(psD[0:1, 0:512] if hf == 0 else psC[0:1, 0:512],
                                 ones_col_f[:, :], sq_s[:, :],
                                 start=(kc == 0), stop=(kc == NKC - 1)))

    @block.scalar
    def _(sc):
        _wait(sc, sPE)
        sAC.inc(sc.activation(out=var_row[:, 0:512], in_=psD[0:1, 0:512], func=AF.Copy))
        sAC.inc(sc.activation(out=var_row[:, 512:1024], in_=psC[0:1, 0:512], func=AF.Copy))
        sc.drain()
        sAC.inc(sc.activation(out=rstd_row[:, :], in_=var_row[:, :], func=AF.Ln,
                              scale=1.0 / D, bias=epsr_1[:, :]))
        sc.drain()
        sAC.inc(sc.activation(out=rstd_row[:, :], in_=rstd_row[:, :], func=AF.Exp,
                              scale=-0.5))

    @block.tensor
    def _(t):
        _wait(t, sAC)
        t.matmul(psD[:, 0:512], ones_row_f[:, :], rstd_row[:, 0:512], start=True, stop=True)
        sPE.inc(t.matmul(psC[:, 0:512], ones_row_f[:, :], rstd_row[:, 512:1024],
                         start=True, stop=True))

    @block.scalar
    def _(sc):
        _wait(sc, sPE)
        sAC.inc(sc.activation(out=rstd_bc[:, 0:512], in_=psD[:, 0:512], func=AF.Copy))
        sAC.inc(sc.activation(out=rstd_bc[:, 512:1024], in_=psC[:, 0:512], func=AF.Copy))

    for kc in range(NKC):
        @block.sync
        def _(sy, kc=kc):
            if kc >= 2:
                _wait(sy, sDV)
            else:
                _wait(sy, sAC)
            sLD.inc(sy.dma_start(out=att_ld[kc % 2][:, :],
                                 in_=attn_dram[kc * 128:(kc + 1) * 128, :]), 16)
            sLD.inc(sy.dma_start(out=gt_ld[kc % 2][:, :],
                                 in_=gate_dram[kc * 128:(kc + 1) * 128, :]), 16)

        @block.vector
        def _(v, kc=kc):
            _wait(v, sLD)
            _wait(v, sAC)
            sDV.inc(v.scalar_tensor_tensor(out=gin_f[:, :], in0=att_ld[kc % 2][:, :],
                                           scalar=normw_col[:, kc:kc + 1],
                                           in1=rstd_bc[:, :], op0=ALU.mult, op1=ALU.mult))
            v.drain()
            sDV.inc(v.tensor_tensor(out=gin[:, kc * T:(kc + 1) * T], in0=gin_f[:, :],
                                    in1=gt_ld[kc % 2][:, :], op=ALU.mult))

    wo_ready = [None] * 4
    wo_free = [None] * 4

    def emit_wo_load(nb):
        @block.gpsimd
        def _(g, nb=nb):
            if nb >= 2:
                _wait(g, sPE, wo_free[nb - 2])
            sGP.inc(g.dma_start(
                out=wo_sl[nb % 2].ap().rearrange("p (kc c) -> p kc c", kc=NKC),
                in_=wo_t[:, nb * 512:(nb + 1) * 512].rearrange("(kc p) c -> p kc c", p=128)), 16)
        wo_ready[nb] = sGP.n

    emit_wo_load(0)
    emit_wo_load(1)

    sm_ac = {}
    sm_pe = {}
    sm_st = {}
    for nb in range(4):
        @block.tensor
        def _(t, nb=nb):
            _wait(t, sGP, wo_ready[nb])
            if nb == 0:
                _wait(t, sDV)
                _wait(t, sAC)
            for ttv in range(NT):
                g = nb * NT + ttv
                ps = psA if g % 2 == 0 else psB
                if g >= 2:
                    _wait(t, sAC, sm_ac[g - 2])
                for kc in range(NKC):
                    sPE.inc(t.matmul(
                        ps[:, :],
                        gin[:, kc * T + ttv * 128: kc * T + (ttv + 1) * 128],
                        wo_sl[nb % 2][:, kc * 512:(kc + 1) * 512],
                        start=(kc == 0), stop=(kc == NKC - 1)))
                sm_pe[g] = sPE.n

                @block.scalar
                def _(sc, nb=nb, ttv=ttv, g=g, ps=ps):
                    _wait(sc, sPE, sm_pe[g])
                    if g >= 2:
                        _wait(sc, sST, sm_st[g - 2])
                    sAC.inc(sc.activation(out=s_stage[g % 2][:, :], in_=ps[:, :],
                                          func=AF.Copy))
                    sm_ac[g] = sAC.n

                @block.sync
                def _(sy, nb=nb, ttv=ttv, g=g):
                    _wait(sy, sAC, sm_ac[g])
                    sST.inc(sy.dma_start(
                        out=s_dram[ttv * 128:(ttv + 1) * 128, nb * 512:(nb + 1) * 512],
                        in_=s_stage[g % 2][:, :]), 16)
                sm_st[g] = sST.n
        wo_free[nb] = sPE.n
        if nb + 2 < 4:
            emit_wo_load(nb + 2)

    if DEBUG:
        @block.sync
        def _(sy):
            _wait(sy, sST)
            sST.inc(sy.dma_start(out=dbg["dbg_s"], in_=s_dram[:, :]), 16)

    phase_barrier(block, ALLSEMS)
    p3b.close()

    if BUILD_PHASES < 6:
        return _finish(nc, ctx, block, out_tm, x_in, sLD, sST)

    p6 = ExitStack()
    e6 = p6.enter_context
    wbufA = e6(nc.sbuf_tensor("wbufA", [128, NKC * NI], BF16))
    wbufB = e6(nc.sbuf_tensor("wbufB", [128, NKC * NI], BF16))
    xg_tm = [e6(nc.sbuf_tensor(f"xg_tm{i}", [128, D], BF16)) for i in range(2)]
    xg_fm = e6(nc.sbuf_tensor("xg_fm", [128, NKC * CAP], BF16))
    g_sb = e6(nc.sbuf_tensor("g_sb", [128, 8 * CAP], BF16))
    u_sb = e6(nc.sbuf_tensor("u_sb", [128, 8 * CAP], BF16))
    dn_sb = e6(nc.sbuf_tensor("dn_sb", [128, 8 * CAP], BF16))
    clip_s = [e6(nc.sbuf_tensor(f"clip_s{i}", [128, CAP], F32)) for i in range(2)]
    y_tm = [e6(nc.sbuf_tensor(f"y_tm{i}", [128, D], F32)) for i in range(2)]

    bufs = [wbufA, wbufB]
    wready = [None] * (3 * NE)
    wfree = [None] * (3 * NE)

    def emit_w_load(si):
        ex = si // 3
        kindw = si % 3
        wd = [eg_t, eu_t, ed_t][kindw]

        @block.gpsimd
        def _(g, si=si, ex=ex, kindw=kindw, wd=wd):
            if si >= 2:
                _wait(g, sPE, wfree[si - 2])
            src = wd[ex]
            kc_n = NKC if kindw < 2 else 8
            sGP.inc(g.dma_start(
                out=bufs[si % 2].ap().rearrange("p (kc c) -> p kc c", kc=kc_n),
                in_=src.rearrange("(kc p) c -> p kc c", p=128)), 16)
        wready[si] = sGP.n

    emit_w_load(0)
    emit_w_load(1)

    tr_ac = {}
    tr_pe = {}
    tr_ac_last = {}
    tr_pe_last = {}
    grp_pe = {}
    grp_dv = {}
    grp_ac = {}
    dn_dv = {}
    dwn_pe = {}
    dwn_ac = {}
    y_st = {}
    ld_mark = {}

    for ex in range(NE):
        @block.sync
        def _(sy, ex=ex):
            if ex == 0:
                _wait(sy, sGP, scat_total)
            else:
                _wait(sy, sPE, tr_pe_last[ex - 1])
            for i in range(2):
                sLD.inc(sy.dma_start(
                    out=xg_tm[i][:, :],
                    in_=xg_dram[ex * CAP + i * 128: ex * CAP + (i + 1) * 128, :]), 16)
        ld_mark[ex] = sLD.n

        @block.tensor
        def _(t, ex=ex):
            _wait(t, sLD, ld_mark[ex])
            for kc in range(NKC):
                for i in range(2):
                    k = kc * 2 + i
                    if (ex, k - 2) in tr_ac:
                        _wait(t, sAC, tr_ac[(ex, k - 2)])
                    elif k < 2 and (ex - 1, 30 + k) in tr_ac:
                        _wait(t, sAC, tr_ac[(ex - 1, 30 + k)])
                    pst = psT if k % 2 == 0 else psT2
                    sPE.inc(t.transpose(pst[:, 0:128],
                                        xg_tm[i][:, kc * 128:(kc + 1) * 128], ident_bf[:, :]))
                    tr_pe[(ex, k)] = sPE.n

                    @block.scalar
                    def _(sc, ex=ex, kc=kc, i=i, k=k, pst=pst):
                        _wait(sc, sPE, tr_pe[(ex, k)])
                        sAC.inc(sc.activation(
                            out=xg_fm[:, kc * CAP + i * 128: kc * CAP + (i + 1) * 128],
                            in_=pst[:, 0:128], func=AF.Copy))
                        tr_ac[(ex, k)] = sAC.n
        tr_pe_last[ex] = sPE.n
        tr_ac_last[ex] = tr_ac[(ex, 31)]

        for gu in range(2):
            si = ex * 3 + gu
            buf = bufs[si % 2]
            dst = g_sb if gu == 0 else u_sb

            @block.tensor
            def _(t, ex=ex, gu=gu, si=si, buf=buf, dst=dst):
                _wait(t, sGP, wready[si])
                if gu == 0:
                    _wait(t, sAC, tr_ac_last[ex])
                for ic in range(8):
                    ps = psA if ic % 2 == 0 else psB
                    key = (si, ic - 2)
                    if key in grp_dv:
                        _wait(t, sDV, grp_dv[key])
                    elif ic < 2 and gu == 1 and (si - 1, 6 + ic) in grp_dv:
                        _wait(t, sDV, grp_dv[(si - 1, 6 + ic)])
                    elif ic < 2 and gu == 0 and ex >= 1 and (ex - 1, 6 + ic) in dwn_ac:
                        _wait(t, sAC, dwn_ac[(ex - 1, 6 + ic)])
                    for kc in range(NKC):
                        sPE.inc(t.matmul(
                            ps[:, 0:CAP],
                            buf[:, kc * NI + ic * 128: kc * NI + (ic + 1) * 128],
                            xg_fm[:, kc * CAP:(kc + 1) * CAP],
                            start=(kc == 0), stop=(kc == NKC - 1)))
                    grp_pe[(si, ic)] = sPE.n

                    @block.vector
                    def _(v, si=si, ic=ic, ps=ps, gu=gu, ex=ex):
                        _wait(v, sPE, grp_pe[(si, ic)])
                        key = (si, ic - 2)
                        if key in grp_ac:
                            _wait(v, sAC, grp_ac[key])
                        elif ic < 2 and gu == 1 and (si - 1, 6 + ic) in grp_ac:
                            _wait(v, sAC, grp_ac[(si - 1, 6 + ic)])
                        elif ic < 2 and gu == 0 and ex >= 1:
                            _wait(v, sAC, grp_ac[((ex - 1) * 3 + 1, 6 + ic)])
                        sDV.inc(v.tensor_scalar(out=clip_s[ic % 2][:, :], in0=ps[:, 0:CAP],
                                                scalar1=ALPHA, scalar2=-ALPHA,
                                                op0=ALU.min, op1=ALU.max))
                        grp_dv[(si, ic)] = sDV.n

                    @block.scalar
                    def _(sc, si=si, gu=gu, ic=ic, dst=dst):
                        _wait(sc, sDV, grp_dv[(si, ic)])
                        sAC.inc(sc.activation(out=dst[:, ic * CAP:(ic + 1) * CAP],
                                              in_=clip_s[ic % 2][:, :],
                                              func=(AF.Silu if gu == 0 else AF.Copy)))
                        grp_ac[(si, ic)] = sAC.n

                    if gu == 1:
                        @block.vector
                        def _(v, ex=ex, si=si, ic=ic):
                            _wait(v, sAC, grp_ac[(si, ic)])
                            if ic == 0 and ex >= 1:
                                _wait(v, sPE, dwn_pe[(ex - 1, 7)])
                            sDV.inc(v.tensor_tensor(
                                out=dn_sb[:, ic * CAP:(ic + 1) * CAP],
                                in0=g_sb[:, ic * CAP:(ic + 1) * CAP],
                                in1=u_sb[:, ic * CAP:(ic + 1) * CAP], op=ALU.mult))
                            if ic == 7:
                                dn_dv[ex] = sDV.n
            wfree[si] = sPE.n
            if si + 2 < 3 * NE:
                emit_w_load(si + 2)

        si = ex * 3 + 2
        buf = bufs[si % 2]

        @block.tensor
        def _(t, ex=ex, si=si, buf=buf):
            _wait(t, sGP, wready[si])
            _wait(t, sDV, dn_dv[ex])
            for sc_i in range(2):
                for ncol in range(4):
                    g = sc_i * 4 + ncol
                    ps = psA if g % 2 == 0 else psB
                    if (ex, g - 2) in dwn_ac:
                        _wait(t, sAC, dwn_ac[(ex, g - 2)])
                    for ic in range(8):
                        sPE.inc(t.matmul(
                            ps[:, :],
                            dn_sb[:, ic * CAP + sc_i * 128: ic * CAP + (sc_i + 1) * 128],
                            buf[:, ic * D + ncol * 512: ic * D + (ncol + 1) * 512],
                            start=(ic == 0), stop=(ic == 7)))
                    dwn_pe[(ex, g)] = sPE.n

                    @block.scalar
                    def _(s2, ex=ex, sc_i=sc_i, ncol=ncol, g=g, ps=ps):
                        _wait(s2, sPE, dwn_pe[(ex, g)])
                        if g == 0 and ex >= 1:
                            _wait(s2, sST, y_st[ex - 1])
                        sAC.inc(s2.activation(out=y_tm[sc_i][:, ncol * 512:(ncol + 1) * 512],
                                              in_=ps[:, :], func=AF.Copy))
                        dwn_ac[(ex, g)] = sAC.n
        wfree[si] = sPE.n
        if si + 2 < 3 * NE:
            emit_w_load(si + 2)

        @block.sync
        def _(sy, ex=ex):
            _wait(sy, sAC, dwn_ac[(ex, 7)])
            for i in range(2):
                sST.inc(sy.dma_start(
                    out=y_dram[ex * CAP + i * 128: ex * CAP + (i + 1) * 128, :],
                    in_=y_tm[i][:, :]), 16)
        y_st[ex] = sST.n

    if DEBUG:
        @block.sync
        def _(sy):
            _wait(sy, sST)
            sST.inc(sy.dma_start(out=dbg["dbg_y"], in_=y_dram[:, :]), 16)

    phase_barrier(block, ALLSEMS)
    p6.close()

    if BUILD_PHASES < 7:
        return _finish(nc, ctx, block, out_tm, x_in, sLD, sST)

    p7o = ExitStack()
    acc = p7o.enter_context(nc.sbuf_tensor("acc", [128, NT * D], F32))

    p7 = ExitStack()
    e7 = p7.enter_context
    wsh = e7(nc.sbuf_tensor("wsh", [128, NKC * NI], BF16))
    xn_res = e7(nc.sbuf_tensor("xn_res", [128, NKC * T], BF16))
    gsh = e7(nc.sbuf_tensor("gsh", [128, 8 * T], BF16))
    ush = e7(nc.sbuf_tensor("ush", [128, 8 * T], BF16))
    dnsh = e7(nc.sbuf_tensor("dnsh", [128, 8 * T], BF16))
    clip_sh = e7(nc.sbuf_tensor("clip_sh", [128, 512], F32))

    @block.sync
    def _(sy):
        for kc in range(NKC):
            sLD.inc(sy.dma_start(out=xn_res[:, kc * T:(kc + 1) * T],
                                 in_=xnfm_dram[kc * 128:(kc + 1) * 128, :]), 16)

    sh_ready = {}
    for gu in range(2):
        wd = sg_t if gu == 0 else su_t
        dst = gsh if gu == 0 else ush

        @block.gpsimd
        def _(g, gu=gu, wd=wd):
            if gu > 0:
                _wait(g, sPE)
            sGP.inc(g.dma_start(
                out=wsh.ap().rearrange("p (kc c) -> p kc c", kc=NKC),
                in_=wd.rearrange("(kc p) c -> p kc c", p=128)), 16)
        sh_ready[gu] = sGP.n

        for ic in range(8):
            for th in range(2):
                ps = psA if (ic * 2 + th) % 2 == 0 else psB

                @block.tensor
                def _(t, gu=gu, ic=ic, th=th, ps=ps):
                    _wait(t, sGP, sh_ready[gu])
                    _wait(t, sLD)
                    _wait(t, sDV)
                    for kc in range(NKC):
                        sPE.inc(t.matmul(
                            ps[:, :],
                            wsh[:, kc * NI + ic * 128: kc * NI + (ic + 1) * 128],
                            xn_res[:, kc * T + th * 512: kc * T + (th + 1) * 512],
                            start=(kc == 0), stop=(kc == NKC - 1)))

                @block.vector
                def _(v, gu=gu, ic=ic, th=th, ps=ps):
                    _wait(v, sPE)
                    _wait(v, sAC)
                    sDV.inc(v.tensor_scalar(out=clip_sh[:, :], in0=ps[:, :],
                                            scalar1=ALPHA, scalar2=-ALPHA,
                                            op0=ALU.min, op1=ALU.max))

                @block.scalar
                def _(sc, gu=gu, ic=ic, th=th, dst=dst):
                    _wait(sc, sDV)
                    sAC.inc(sc.activation(
                        out=dst[:, ic * T + th * 512: ic * T + (th + 1) * 512],
                        in_=clip_sh[:, :], func=(AF.Silu if gu == 0 else AF.Copy)))

    @block.vector
    def _(v):
        _wait(v, sAC)
        for ic in range(8):
            sDV.inc(v.tensor_tensor(out=dnsh[:, ic * T:(ic + 1) * T],
                                    in0=gsh[:, ic * T:(ic + 1) * T],
                                    in1=ush[:, ic * T:(ic + 1) * T], op=ALU.mult))

    @block.gpsimd
    def _(g):
        _wait(g, sDV)
        sGP.inc(g.dma_start(
            out=wsh.ap().rearrange("p (kc c) -> p kc c", kc=8),
            in_=sd_t.rearrange("(kc p) c -> p kc c", p=128)), 16)
    sh_ready[2] = sGP.n

    for ttv in range(NT):
        for ncol in range(4):
            ps = psA if (ttv * 4 + ncol) % 2 == 0 else psB

            @block.tensor
            def _(t, ttv=ttv, ncol=ncol, ps=ps):
                _wait(t, sGP, sh_ready[2])
                _wait(t, sAC)
                for ic in range(8):
                    sPE.inc(t.matmul(
                        ps[:, :],
                        dnsh[:, ic * T + ttv * 128: ic * T + (ttv + 1) * 128],
                        wsh[:, ic * D + ncol * 512: ic * D + (ncol + 1) * 512],
                        start=(ic == 0), stop=(ic == 7)))

            @block.scalar
            def _(sc, ttv=ttv, ncol=ncol, ps=ps):
                _wait(sc, sPE)
                sAC.inc(sc.activation(
                    out=acc[:, ttv * D + ncol * 512: ttv * D + (ncol + 1) * 512],
                    in_=ps[:, :], func=AF.Copy))

    phase_barrier(block, ALLSEMS)
    p7.close()

    p8 = ExitStack()
    ybuf = [p8.enter_context(nc.sbuf_tensor(f"ybuf{i}", [128, D], F32)) for i in range(2)]
    acc_t1 = p8.enter_context(nc.sbuf_tensor("acc_t1", [128, D], F32))
    acc_done = sAC.n

    for tt in range(NT):
        for j in range(TOPK):
            it = tt * TOPK + j

            @block.gpsimd
            def _(g, tt=tt, j=j, it=it):
                if it == 0:
                    _wait(g, sST)
                if it >= 2:
                    _wait(g, sDV)
                sGP.inc(g.indirect_dma_start(
                    out=ybuf[it % 2][:, :], out_offset=None,
                    in_=y_dram[:, :],
                    in_offset=bass.IndirectOffsetOnAxis(
                        ap=gidx_g[:, tt * TOPK + j: tt * TOPK + j + 1], axis=0),
                    bounds_check=breg["gath"], oob_is_err=False), 16)

            @block.vector
            def _(v, tt=tt, j=j, it=it):
                _wait(v, sGP)
                if it == 0:
                    _wait(v, sAC, acc_done)
                dst = acc_t1[:, :] if j % 2 == 0 else acc[:, tt * D:(tt + 1) * D]
                srcb = acc[:, tt * D:(tt + 1) * D] if j % 2 == 0 else acc_t1[:, :]
                sDV.inc(v.scalar_tensor_tensor(
                    out=dst,
                    in0=ybuf[it % 2][:, :],
                    scalar=pj_sb[:, tt * TOPK + j: tt * TOPK + j + 1],
                    in1=srcb, op0=ALU.mult, op1=ALU.add))
                v.drain()

    phase_barrier(block, ALLSEMS)
    p8.close()

    p9 = ExitStack()
    e9 = p9.enter_context
    g_f = e9(nc.sbuf_tensor("g_f", [128, D], F32))
    prod9 = e9(nc.sbuf_tensor("prod9", [128, D], F32))
    x_ld = [e9(nc.sbuf_tensor(f"x_ld{i}", [128, D], F32)) for i in range(2)]
    s_ld = [e9(nc.sbuf_tensor(f"s_ld{i}", [128, D], BF16)) for i in range(2)]
    o_st = [e9(nc.sbuf_tensor(f"o_st{i}", [128, D], F32)) for i in range(2)]

    for tt in range(NT):
        @block.sync
        def _(sy, tt=tt):
            if tt >= 2:
                _wait(sy, sDV)
            sLD.inc(sy.dma_start(out=x_ld[tt % 2][:, :],
                                 in_=x_in[tt * 128:(tt + 1) * 128, :]), 16)
            sLD.inc(sy.dma_start(out=s_ld[tt % 2][:, :],
                                 in_=s_dram[tt * 128:(tt + 1) * 128, :]), 16)

        @block.scalar
        def _(sc, tt=tt):
            _wait(sc, sDV)
            if tt >= 1:
                _wait(sc, sAC)
            sAC.inc(sc.activation(out=g_f[:, :], in_=acc[:, tt * D:(tt + 1) * D],
                                  func=AF.Silu))

        @block.vector
        def _(v, tt=tt):
            _wait(v, sAC)
            _wait(v, sLD)
            _wait(v, sST)
            sDV.inc(v.tensor_tensor(out=prod9[:, :], in0=g_f[:, :], in1=s_ld[tt % 2][:, :],
                                    op=ALU.mult))
            v.drain()
            sDV.inc(v.tensor_tensor(out=o_st[tt % 2][:, :], in0=prod9[:, :],
                                    in1=x_ld[tt % 2][:, :], op=ALU.add))

        @block.sync
        def _(sy, tt=tt):
            _wait(sy, sDV)
            sST.inc(sy.dma_start(out=out_tm[tt * 128:(tt + 1) * 128, :],
                                 in_=o_st[tt % 2][:, :]), 16)

    @block.sync
    def _(sy):
        _wait(sy, sST)

    p9.close()
    p7o.close()
    ctx.close()
    return nc


def _finish(nc, ctx, block, out_tm, x_in, sLD, sST):
    with nc.sbuf_tensor("fin_buf", [128, D], F32) as buf:
        @block.sync
        def _(sy):
            _wait(sy, sST)
            _wait(sy, sLD)
            for tt in range(NT):
                sLD.inc(sy.dma_start(out=buf[:, :], in_=x_in[tt * 128:(tt + 1) * 128, :]), 16)
                _wait(sy, sLD)
                sST.inc(sy.dma_start(out=out_tm[tt * 128:(tt + 1) * 128, :], in_=buf[:, :]), 16)
            _wait(sy, sST)
    ctx.close()
    return nc


_CACHE = {}


def _get_nc():
    key = (BUILD_PHASES, DEBUG)
    if key not in _CACHE:
        _CACHE[key] = build_nc()
    return _CACHE[key]


def host_prep(inputs):
    x = np.asarray(inputs["L0_attn_hidden_states"])[0]
    slope = np.ascontiguousarray(np.asarray(inputs["L0_attn_slope_rate"]).reshape(NH, 1))
    wqkv = np.asarray(inputs["L0_attn_w_qkv"]).reshape(NH, 3, HD, D)
    wq_t = np.ascontiguousarray(wqkv[:, 0].reshape(NH * HD, D).T)
    wk_t = np.ascontiguousarray(wqkv[:, 1].reshape(NH * HD, D).T)
    wv_t = np.ascontiguousarray(wqkv[:, 2].reshape(NH * HD, D).T)
    wgate_t = np.ascontiguousarray(np.asarray(inputs["L0_attn_w_gate"]).T)
    wo_t = np.ascontiguousarray(np.asarray(inputs["L0_attn_w_out"]).T)
    normw = np.ascontiguousarray(np.asarray(inputs["L0_attn_norm_w"]).reshape(NKC, 128))
    router_t = np.ascontiguousarray(np.asarray(inputs["L0_ffn_router_w"]).T)
    ebias = np.ascontiguousarray(np.asarray(inputs["L0_ffn_expert_bias"]).reshape(1, NE))
    eg_t = np.ascontiguousarray(np.asarray(inputs["L0_ffn_gate_w"]).transpose(0, 2, 1))
    eu_t = np.ascontiguousarray(np.asarray(inputs["L0_ffn_up_w"]).transpose(0, 2, 1))
    ed_t = np.ascontiguousarray(np.asarray(inputs["L0_ffn_down_w"]).transpose(0, 2, 1))
    sg_t = np.ascontiguousarray(np.asarray(inputs["L0_ffn_shared_gate_w"]).T)
    su_t = np.ascontiguousarray(np.asarray(inputs["L0_ffn_shared_up_w"]).T)
    sd_t = np.ascontiguousarray(np.asarray(inputs["L0_ffn_shared_down_w"]).T)

    in_maps = []
    for c in range(NCORES):
        ec = np.full((NH, NCORES), 1e9, np.float32)
        for cp in range(c):
            ec[:, cp] = 1024.0 * (c - 1 - cp)
        in_maps.append({
            "x": np.ascontiguousarray(x[c * T:(c + 1) * T]),
            "slope": slope, "wq_t": wq_t, "wk_t": wk_t, "wv_t": wv_t,
            "wgate_t": wgate_t, "wo_t": wo_t, "normw": normw,
            "router_t": router_t, "ebias": ebias,
            "eg_t": eg_t, "eu_t": eu_t, "ed_t": ed_t,
            "sg_t": sg_t, "su_t": su_t, "sd_t": sd_t,
            "e_const": ec,
        })
    return in_maps


def kernel(**inputs):
    nc = _get_nc()
    in_maps = host_prep(inputs)
    res = run_bass_kernel_spmd(nc, in_maps, core_ids=list(range(NCORES)))
    out = np.concatenate([r["out_tm"] for r in res.results], axis=0)
    if DEBUG:
        kernel.last_results = res
    return out.reshape(1, NCORES * T, D).astype(np.float32)
